# version 10
# baseline (speedup 1.0000x reference)
import numpy as np
import ml_dtypes
import concourse.bass as bass
import concourse.mybir as mybir
from concourse.bass_utils import run_bass_kernel_spmd
from contextlib import ExitStack

F32 = mybir.dt.float32
BF16 = mybir.dt.bfloat16
FP16 = mybir.dt.float16
U8 = mybir.dt.uint8
I16 = mybir.dt.int16
ALU = mybir.AluOpType
AF = mybir.ActivationFunctionType
AX = mybir.AxisListType

T = 4096
D = 1024
NT = T // 128
NC_DC = D // 128
I_FFN = 2816
M05, M1, M2 = 6291456.0, 12582912.0, 25165824.0
QMID = (0.25, 0.75, 1.25, 1.75, 2.5, 3.5, 5.0)
QINC = (0.5, 0.5, 0.5, 0.5, 1.0, 1.0, 2.0)
EPS_LN = 1e-5
EPS_ATT = 1e-6

ENGS = ("tensor", "vector", "scalar", "gpsimd", "sync")
HAS_GP_REDUCE = False


class B:

    def __init__(self, nc):
        self.nc = nc
        self.ops = {e: [] for e in ENGS}
        self.stack = ExitStack()
        self.sem = {}
        self.cnt = {}
        self.hwm = {e: {} for e in ENGS}
        for e in ENGS:
            self.sem[e] = self.stack.enter_context(nc.semaphore("sem_" + e))
            self.cnt[e] = 0
        self.NDMA = 8
        self.dmac = {}
        for q in ("dsync", "dgp", "dscal"):
            for i in range(self.NDMA):
                s = f"{q}{i}"
                self.sem[s] = self.stack.enter_context(nc.semaphore("sem_" + s))
                self.cnt[s] = 0
            self.dmac[q] = 0

    def close(self):
        self.stack.close()

    def _wait(self, eng, tok, force=False):
        if tok is None:
            return
        sname, val = tok
        if val <= 0:
            return
        if not force and sname == eng == "tensor":
            return
        if self.hwm[eng].get(sname, 0) >= val:
            return
        self.hwm[eng][sname] = val
        sem = self.sem[sname]
        self.ops[eng].append(lambda e, s=sem, v=val: e.wait_ge(s, v))

    def emit(self, eng, fn, deps=(), inc=False):
        for d in deps:
            self._wait(eng, d)
        if inc:
            self.cnt[eng] += 1
            val = self.cnt[eng]
            sem = self.sem[eng]

            def wrapped(e, fn=fn, sem=sem):
                r = fn(e)
                assert r is not None, "inc=True needs fn to return the last instruction"
                r.then_inc(sem, 1)
            self.ops[eng].append(wrapped)
            return (eng, val)
        self.ops[eng].append(fn)
        return None

    def dma(self, qeng, out, in_, deps=()):
        q = {"sync": "dsync", "gpsimd": "dgp", "scalar": "dscal"}[qeng]
        for d in deps:
            self._wait(qeng, d)
        sname = f"{q}{self.dmac[q] % self.NDMA}"
        self.dmac[q] += 1
        self.cnt[sname] += 16
        val = self.cnt[sname]
        sem = self.sem[sname]
        self.ops[qeng].append(lambda e, o=out, i=in_, s=sem: e.dma_start(out=o, in_=i).then_inc(s, 16))
        return (sname, val)

    def tok(self, eng):
        return (eng, self.cnt[eng])

    def rw(self, eng, fn, reads=(), writes=(), extra=(), inc=True):
        assert inc, "rw requires inc=True so Buf tokens are real"

        """Emit with automatic buffer dependency tracking.
        reads/writes are Buf objects; conflicting accesses get sem edges
        (required by the grading simulator's race model even same-engine)."""
        deps = list(extra)
        for bf in reads:
            if bf.w is not None:
                deps.append(bf.w)
        for bf in writes:
            if bf.w is not None:
                deps.append(bf.w)
            deps.extend(bf.readers())
        t = self.emit(eng, fn, deps=deps, inc=inc)
        for bf in reads:
            bf.add_reader(t if t is not None else self.tok(eng))
        for bf in writes:
            bf.w = t if t is not None else self.tok(eng)
            bf.r = {}
        return t

    def dma_rw(self, qeng, out, in_, reads=(), writes=(), extra=()):
        deps = list(extra)
        for bf in reads:
            if bf.w is not None:
                deps.append(bf.w)
        for bf in writes:
            if bf.w is not None:
                deps.append(bf.w)
            deps.extend(bf.readers())
        t = self.dma(qeng, out, in_, deps=deps)
        for bf in reads:
            bf.add_reader(t)
        for bf in writes:
            bf.w = t
            bf.r = {}
        return t

    def barrier(self):
        dtoks = [(f"{q}{i}", self.cnt[f"{q}{i}"]) for q in ("dsync", "dgp", "dscal")
                 for i in range(self.NDMA)]
        for e in ENGS:
            self._wait(e, (e, self.cnt[e]), force=True)
            self.emit(e, lambda eng: eng.nop(), inc=True)
        toks2 = [(e, self.cnt[e]) for e in ENGS]
        for e in ENGS:
            for t in toks2:
                if t[0] != e:
                    self._wait(e, t)
            for t in dtoks:
                self._wait(e, t)

    def finalize(self, block):
        for ename, lst in self.ops.items():
            dec = getattr(block, ename)

            def make(lst):
                def body(eng):
                    for f in lst:
                        f(eng)
                return body
            dec(make(lst))


class Buf:
    __slots__ = ("w", "r")

    def __init__(self):
        self.w = None
        self.r = {}

    def add_reader(self, tok):
        if tok is None:
            return
        s, v = tok
        if self.r.get(s, 0) < v:
            self.r[s] = v

    def readers(self):
        return [(s, v) for s, v in self.r.items()]


def ceil_div(a, b):
    return (a + b - 1) // b


def build(nc, phases):
    dx = nc.dram_tensor("x", [T, D], F32, kind="ExternalInput").ap()
    dout = nc.dram_tensor("out", [T, D], F32, kind="ExternalOutput").ap()
    dw = {}
    for name, shape in [
        ("l0_wq", [D, D]), ("l0_wk", [D, D]), ("l0_wv", [D, D]),
        ("l0_bq", [1, D]), ("l0_bk", [1, D]), ("l0_bv", [1, D]), ("l0_beta", [1, 8]),
        ("f0_win", [D, 64]), ("f0_bin", [1, 64]), ("f0_w1", [1, 1024]), ("f0_b1", [1, 1024]),
        ("f0_w2", [1, 1024]), ("f0_b2", [1, 64]), ("f0_wout", [64, D]), ("f0_bout", [1, D]),
        ("l1_wqt", [D, D]), ("l1_wkt", [D, D]), ("l1_wvt", [D, D]), ("l1_wot", [D, D]),
        ("l1_bq", [1, D]), ("l1_bk", [1, D]), ("l1_bv", [1, D]), ("l1_bo", [1, D]),
        ("f1_wg", [I_FFN, D]), ("f1_wu", [I_FFN, D]), ("f1_wd", [D, I_FFN]),
    ]:
        dw[name] = nc.dram_tensor(name, shape, F32, kind="ExternalInput").ap()
    d_wgq = nc.dram_tensor("wgq", [NC_DC * I_FFN * 128], BF16, kind="Internal").ap()
    d_wuq = nc.dram_tensor("wuq", [NC_DC * I_FFN * 128], BF16, kind="Internal").ap()
    d_wdq = nc.dram_tensor("wdq", [22, 128, D], BF16, kind="Internal").ap()

    b = B(nc)
    sb = lambda name, shape, dt=F32: b.stack.enter_context(nc.sbuf_tensor(name, shape, dt))
    ps = lambda name, shape, dt=F32: b.stack.enter_context(nc.psum_tensor(name, shape, dt))

    stream = sb("stream", [128, NT, D], BF16)
    s_stream = [Buf() for _ in range(NT)]
    ident = sb("ident", [128, 128], BF16); s_ident = Buf()
    lmask = sb("lmask", [128, 128], BF16); s_lmask = Buf()
    g8 = sb("g8", [128, 8]); s_g8 = Buf()
    omg8 = sb("omg8", [128, 8]); s_omg8 = Buf()
    s_iot = Buf()
    ones_col = sb("ones_col", [1, 128], BF16); s_ones = Buf()
    st_s1 = sb("st_s1", [128, NT]); s_s1c = [Buf() for _ in range(NT)]
    st_s2 = sb("st_s2", [128, NT]); s_s2c = [Buf() for _ in range(NT)]
    st_t1 = sb("st_t1", [128, NT]); s_t1 = Buf()
    st_rstd = sb("st_rstd", [128, NT]); s_rstd = Buf()
    st_bias = sb("st_bias", [128, NT]); s_bias = Buf()
    s_s1 = Buf(); s_s2 = Buf()

    INV_SQRT_DK = 1.0 / float(np.sqrt(np.float32(128.0)))
    for i, cv in enumerate([EPS_LN, EPS_ATT, INV_SQRT_DK]):
        cstt = sb(f"cstf{i}", [128, 1], F32)
        b.emit("gpsimd", lambda g, cstt=cstt, cv=cv: g.memset(cstt[:], cv), inc=True)
        nc.const_aps.register(F32, cv, cstt[:])

    w_small, s_small = {}, {}
    for name, shape in [
        ("f0_win", [128, NC_DC, 64]), ("f0_wout", [64, D]),
        ("f0_w1", [128, 1024]), ("f0_b1", [128, 1024]), ("f0_w2", [128, 1024]),
        ("f0_b2", [128, 64]),
        ("f0_bin", [1, 64]), ("f0_bout", [1, D]),
        ("l0_bq", [1, D]), ("l0_bk", [1, D]), ("l0_bv", [1, D]),
        ("l1_bq", [1, D]), ("l1_bk", [1, D]), ("l1_bv", [1, D]), ("l1_bo", [1, D]),
    ]:
        w_small[name] = sb("sw_" + name, shape, BF16)
        s_small[name] = Buf()

    stage32 = sb("stage32", [128, D])
    s_stg = [Buf()]

    import os as _os
    _skipx = _os.environ.get("SKIPX", "0") == "1"
    _skipw = _os.environ.get("SKIPW", "0") == "1"
    for t in (range(0) if _skipx else range(NT)):
        b.dma_rw("sync", stage32[:], dx[t * 128:(t + 1) * 128, :], writes=[s_stg[0]])
        b.rw("gpsimd", lambda g, t=t: g.tensor_copy(out=stream[:, t], in_=stage32[:]),
             reads=[s_stg[0]], writes=[s_stream[t]])

    b.rw("gpsimd", lambda g: g.memset(ones_col[:], 1.0), writes=[s_ones])

    def load_small(name, dramname, rows, cols):
        if _skipw:
            return
        b.dma_rw("sync", stage32[:rows, 0:cols], dw[dramname][0:rows, 0:cols], writes=[s_stg[0]])
        b.rw("vector", lambda v: v.tensor_copy(out=w_small[name][0:rows, 0:cols], in_=stage32[:rows, 0:cols]),
             reads=[s_stg[0]], writes=[s_small[name]])

    for nm, dn, r, c in [("f0_wout", "f0_wout", 64, D), ("f0_bin", "f0_bin", 1, 64),
                         ("f0_bout", "f0_bout", 1, D),
                         ("l0_bq", "l0_bq", 1, D), ("l0_bk", "l0_bk", 1, D), ("l0_bv", "l0_bv", 1, D),
                         ("l1_bq", "l1_bq", 1, D), ("l1_bk", "l1_bk", 1, D), ("l1_bv", "l1_bv", 1, D),
                         ("l1_bo", "l1_bo", 1, D)]:
        load_small(nm, dn, r, c)
    if not _skipw:
      b.dma_rw("sync", stage32[:, 0:512].rearrange("p (c m) -> p c m", m=64),
             dw["f0_win"].rearrange("(c p) m -> p c m", p=128), writes=[s_stg[0]])
    b.rw("vector", lambda v: v.tensor_copy(out=w_small["f0_win"][:], in_=stage32[:, 0:512].rearrange("p (c m) -> p c m", m=64)),
         reads=[s_stg[0]], writes=[s_small["f0_win"]])

    setup_stack = ExitStack()
    ssb = lambda name, shape, dt=F32: setup_stack.enter_context(nc.sbuf_tensor(name, shape, dt, side="right"))
    iot = ssb("iot", [128, 128])
    b.rw("gpsimd", lambda g: g.iota(iot[:], [[1, 128]], base=0, channel_multiplier=-1,
                                    allow_small_or_imprecise_dtypes=True), writes=[s_iot])
    b.rw("vector", lambda v: v.tensor_single_scalar(out=ident[:], in_=iot[:], scalar=0.0, op=ALU.is_equal),
         reads=[s_iot], writes=[s_ident])
    b.rw("vector", lambda v: v.tensor_single_scalar(out=lmask[:], in_=iot[:], scalar=0.0, op=ALU.is_le),
         reads=[s_iot], writes=[s_lmask])
    sps = lambda name, shape, dt=F32: setup_stack.enter_context(nc.psum_tensor(name, shape, dt, side="right"))
    rows32 = {n: ssb("r32_" + n, [1, w]) for n, w in
              [("w1", 1024), ("b1", 1024), ("w2", 1024), ("b2", 64), ("beta", 8)]}
    rows_b = {n: ssb("rb_" + n, [1, w], BF16) for n, w in
              [("w1", 1024), ("b1", 1024), ("w2", 1024), ("b2", 64)]}
    s_rows32 = {n: Buf() for n in rows32}
    s_rows_b = {n: Buf() for n in rows_b}
    for n, dn in [("w1", "f0_w1"), ("b1", "f0_b1"), ("w2", "f0_w2"), ("b2", "f0_b2"), ("beta", "l0_beta")]:
        b.dma_rw("sync", rows32[n][:], dw[dn], writes=[s_rows32[n]])
    for n in ["w1", "b1", "w2", "b2"]:
        b.rw("vector", lambda v, n=n: v.tensor_copy(out=rows_b[n][:], in_=rows32[n][:]),
             reads=[s_rows32[n]], writes=[s_rows_b[n]])
    g_row = ssb("g_row", [1, 8]); s_growf = Buf()
    g_rowb = ssb("g_rowb", [1, 8], BF16); s_growb = Buf()
    b.rw("scalar", lambda s: s.activation(out=g_row[:], in_=rows32["beta"][:], func=AF.Sigmoid),
         reads=[s_rows32["beta"]], writes=[s_growf])
    b.rw("vector", lambda v: v.tensor_copy(out=g_rowb[:], in_=g_row[:]), reads=[s_growf], writes=[s_growb])
    pbc = sps("pbc", [128, 512], F32); s_pbc = Buf()

    def bcast_row(src_row, s_src, width, dst, s_dst):
        for c0 in range(0, width, 512):
            w = min(512, width - c0)
            b.rw("tensor", lambda t, c0=c0, w=w: t.matmul(out=pbc[:, 0:w], lhsT=ones_col[:],
                 rhs=src_row[:, c0:c0 + w], start=True, stop=True),
                 reads=[s_src, s_ones], writes=[s_pbc])
            b.rw("vector", lambda v, c0=c0, w=w: v.tensor_copy(out=dst[:, c0:c0 + w], in_=pbc[:, 0:w]),
                 reads=[s_pbc], writes=[s_dst])

    bcast_row(rows_b["w1"][:], s_rows_b["w1"], 1024, w_small["f0_w1"][:], s_small["f0_w1"])
    bcast_row(rows_b["b1"][:], s_rows_b["b1"], 1024, w_small["f0_b1"][:], s_small["f0_b1"])
    bcast_row(rows_b["w2"][:], s_rows_b["w2"], 1024, w_small["f0_w2"][:], s_small["f0_w2"])
    bcast_row(rows_b["b2"][:], s_rows_b["b2"], 64, w_small["f0_b2"][:], s_small["f0_b2"])
    bcast_row(g_rowb[:], s_growb, 8, g8[:], s_g8)
    b.rw("vector", lambda v: v.tensor_scalar(out=omg8[:], in0=g8[:], scalar1=-1.0, scalar2=1.0,
         op0=ALU.mult, op1=ALU.add), reads=[s_g8], writes=[s_omg8])

    b.barrier()
    setup_stack.close()

    def emit_output():
        last = None
        for t in range(NT):
            b.rw("gpsimd", lambda g, t=t: g.tensor_copy(out=stage32[:], in_=stream[:, t]),
                 reads=[s_stream[t]], writes=[s_stg[0]])
            last = b.dma_rw("sync", dout[t * 128:(t + 1) * 128, :], stage32[:],
                            reads=[s_stg[0]])
        for q in ("dsync", "dgp", "dscal"):
            for i in range(b.NDMA):
                sname = f"{q}{i}"
                if b.cnt[sname] > 0:
                    b._wait("sync", (sname, b.cnt[sname]))

    def emit_hamprobe2():
        fs = ExitStack()
        hw8 = fs.enter_context(nc.sbuf_tensor("hp2_w8", [128, 8, 128], BF16, side="right"))
        ha = fs.enter_context(nc.sbuf_tensor("hp2_a", [128, 512], BF16, side="right"))
        pA = fs.enter_context(nc.psum_tensor("hp2_pa", [128, 512], F32, side="right"))
        pB = fs.enter_context(nc.psum_tensor("hp2_pb", [128, 512], F32, side="right"))
        s_hw = Buf(); s_ha = Buf(); s_qa = Buf(); s_qb = Buf()
        b.rw("gpsimd", lambda g: g.memset(hw8[:], 0.5), writes=[s_hw])
        b.rw("gpsimd", lambda g: g.memset(ha[:], 0.5), writes=[s_ha])
        for g in range(125):
            pp, s_pp = (pA, s_qa) if g % 2 == 0 else (pB, s_qb)
            for c in range(NC_DC):
                b.rw("tensor", lambda e, c=c, pp=pp: e.matmul(
                    out=pp[:], lhsT=hw8[:, c], rhs=ha[:], start=(c == 0), stop=(c == 7)),
                    reads=[s_hw, s_ha], writes=[s_pp])
        b.barrier()
        fs.close()

    if "min2" in phases:
        emit_hamprobe2()
        emit_output()
        return b
    if "identity" in phases or "min" in phases:
        emit_output()
        return b

    pa = [ps(f"pa{i}", [128, 512], F32) for i in range(4)]
    s_pa = [Buf() for _ in range(4)]
    pnk = ps("pnk", [128, 130], F32); s_pnk = Buf()
    pmu = ps("pmu", [128, 130], F32); s_pmu = Buf()
    pt = [ps(f"pt{i}", [128, 128], BF16) for i in range(2)]
    s_pt = [Buf(), Buf()]
    ones_row = sb("ones_row", [1, 512], BF16); s_onesr = Buf()
    b.rw("gpsimd", lambda g: g.memset(ones_row[:], 1.0), writes=[s_onesr])

    def emit_ln_stats(scrA, s_scrA, scrB, s_scrB):
        for t in range(NT):
            p = t % 2
            b.rw("vector", lambda v, t=t, p=p: v.tensor_scalar(
                out=scrA[p], in0=stream[:, t], scalar1=1.0, scalar2=0.0,
                op0=ALU.mult, op1=ALU.add, accum_out=st_s1[:, t:t + 1]),
                reads=[s_stream[t]], writes=[s_scrA[p], s_s1c[t]])
            b.rw("vector", lambda v, t=t, p=p: v.scalar_tensor_tensor(
                out=scrB[p], in0=stream[:, t], scalar=1.0, in1=stream[:, t],
                op0=ALU.mult, op1=ALU.mult, accum_out=st_s2[:, t:t + 1]),
                reads=[s_stream[t]], writes=[s_scrB[p], s_s2c[t]])
        b.rw("vector", lambda v: v.tensor_scalar(
            out=st_t1[:], in0=st_s1[:], scalar1=1.0 / D, scalar2=0.0, op0=ALU.mult, op1=ALU.add),
            reads=s_s1c, writes=[s_t1])
        b.rw("vector", lambda v: v.tensor_tensor(out=st_bias[:], in0=st_t1[:], in1=st_t1[:], op=ALU.mult),
            reads=[s_t1], writes=[s_bias])
        b.rw("vector", lambda v: v.scalar_tensor_tensor(
            out=st_s2[:], in0=st_s2[:], scalar=1.0 / D, in1=st_bias[:], op0=ALU.mult, op1=ALU.subtract),
            reads=s_s2c + [s_bias], writes=[s_s2] + s_s2c)
        b.rw("scalar", lambda s: s.activation(out=st_s1[:], in_=st_s2[:], func=AF.Sqrt, bias=EPS_LN, scale=1.0),
            reads=[s_s2], writes=[s_s1] + s_s1c)
        b.rw("vector", lambda v: v.reciprocal(out=st_rstd[:], in_=st_s1[:]),
            reads=[s_s1], writes=[s_rstd])
        b.rw("vector", lambda v: v.scalar_tensor_tensor(
            out=st_bias[:], in0=st_t1[:], scalar=-1.0, in1=st_rstd[:], op0=ALU.mult, op1=ALU.mult),
            reads=[s_t1, s_rstd], writes=[s_bias])

    def emit_xhat(t, dst, s_dst):
        b.rw("scalar", lambda s, t=t: s.activation(
            out=dst, in_=stream[:, t], func=AF.Identity,
            bias=st_bias[:, t:t + 1], scale=st_rstd[:, t:t + 1]),
            reads=[s_stream[t], s_bias, s_rstd], writes=[s_dst])

    _f0_call = [0]

    def emit_ffn0():
        _f0_call[0] += 1
        sfx = f"_c{_f0_call[0]}"
        fs = ExitStack()
        fsb = lambda name, shape, dt=F32: fs.enter_context(nc.sbuf_tensor(name, shape, dt, side="right"))
        f0_xhat = [fsb(f"f0_xhat{i}" + sfx, [128, D], BF16) for i in range(2)]
        f0_xf = [fsb(f"f0_xf{i}" + sfx, [128, NC_DC, 128], BF16) for i in range(2)]
        f0_ub = [fsb(f"f0_ub{i}" + sfx, [128, 1024], BF16) for i in range(2)]
        f0_hb = [fsb(f"f0_hb{i}" + sfx, [128, 1024], BF16) for i in range(2)]
        f0_g32 = [fsb(f"f0_g32{i}" + sfx, [128, 64]) for i in range(2)]
        f0_g2b = [fsb(f"f0_g2b{i}" + sfx, [128, 64], BF16) for i in range(2)]
        f0_gfm = [fsb(f"f0_gfm{i}" + sfx, [64, 128], BF16) for i in range(2)]
        sf0 = {n: [Buf(), Buf()] for n in ("xhat", "xf", "ub", "hb", "g32", "g2b", "gfm")}
        emit_ln_stats([f0_ub[0][:], f0_ub[1][:]], sf0["ub"], [f0_hb[0][:], f0_hb[1][:]], sf0["hb"])
        for t in range(NT):
            p = t % 2
            emit_xhat(t, f0_xhat[p][:], sf0["xhat"][p])
            for c in range(NC_DC):
                pc = c % 2
                b.rw("tensor", lambda e, c=c, p=p, pc=pc: e.transpose(
                    out=pt[pc][:], in_=f0_xhat[p][:, c * 128:(c + 1) * 128], identity=ident[:]),
                    reads=[sf0["xhat"][p], s_ident], writes=[s_pt[pc]])
                b.rw("scalar", lambda s, c=c, p=p, pc=pc: s.copy(
                    out=f0_xf[p][:, c], in_=pt[pc][:]),
                    reads=[s_pt[pc]], writes=[sf0["xf"][p]])
            for c in range(NC_DC):
                b.rw("tensor", lambda e, c=c, p=p: e.matmul(
                    out=pa[p][:, 0:64], lhsT=f0_xf[p][:, c], rhs=w_small["f0_win"][:, c],
                    start=(c == 0), stop=False),
                    reads=[sf0["xf"][p], s_small["f0_win"]], writes=[s_pa[p]])
            b.rw("tensor", lambda e, p=p: e.matmul(
                out=pa[p][:, 0:64], lhsT=ones_col[:], rhs=w_small["f0_bin"][0:1, :],
                start=False, stop=True),
                reads=[s_ones, s_small["f0_bin"]], writes=[s_pa[p]])
            s_bv = pa[p][:, 0:64].rearrange("q (m o) -> q m o", o=1).broadcast_to([128, 64, 16])
            b.rw("vector", lambda v, p=p, s_bv=s_bv: v.tensor_tensor(
                out=f0_ub[p][:].rearrange("q (m k) -> q m k", k=16), in0=s_bv,
                in1=w_small["f0_w1"][:].rearrange("q (m k) -> q m k", k=16), op=ALU.mult),
                reads=[s_pa[p], s_small["f0_w1"]], writes=[sf0["ub"][p]])
            b.rw("vector", lambda v, p=p: v.tensor_tensor(
                out=f0_hb[p][:], in0=f0_ub[p][:], in1=w_small["f0_b1"][:], op=ALU.add),
                reads=[sf0["ub"][p], s_small["f0_b1"]], writes=[sf0["hb"][p]])
            b.rw("vector", lambda v, p=p: v.tensor_single_scalar(
                out=f0_hb[p][:], in_=f0_hb[p][:], scalar=0.0, op=ALU.max),
                reads=[sf0["hb"][p]], writes=[sf0["hb"][p]])
            b.rw("vector", lambda v, p=p: v.tensor_tensor(
                out=f0_ub[p][:], in0=f0_hb[p][:], in1=w_small["f0_w2"][:], op=ALU.mult),
                reads=[sf0["hb"][p], s_small["f0_w2"]], writes=[sf0["ub"][p]])
            b.rw("vector", lambda v, p=p: v.tensor_reduce(
                out=f0_g32[p][:], in_=f0_ub[p][:].rearrange("q (m k) -> q m k", k=16),
                axis=AX.X, op=ALU.add),
                reads=[sf0["ub"][p]], writes=[sf0["g32"][p]])
            b.rw("vector", lambda v, p=p: v.tensor_tensor(
                out=f0_g2b[p][:], in0=f0_g32[p][:], in1=w_small["f0_b2"][:], op=ALU.add),
                reads=[sf0["g32"][p], s_small["f0_b2"]], writes=[sf0["g2b"][p]])
            pc = t % 2
            b.rw("tensor", lambda e, p=p, pc=pc: e.transpose(
                out=pt[pc][0:64, :], in_=f0_g2b[p][:], identity=ident[:]),
                reads=[sf0["g2b"][p], s_ident], writes=[s_pt[pc]])
            b.rw("scalar", lambda s, p=p, pc=pc: s.copy(out=f0_gfm[p][:], in_=pt[pc][0:64, :]),
                reads=[s_pt[pc]], writes=[sf0["gfm"][p]])
            for o in range(2):
                pq = 2 + o
                b.rw("tensor", lambda e, p=p, o=o, pq=pq: e.matmul(
                    out=pa[pq][:], lhsT=f0_gfm[p][:],
                    rhs=w_small["f0_wout"][0:64, o * 512:(o + 1) * 512], start=True, stop=False),
                    reads=[sf0["gfm"][p], s_small["f0_wout"]], writes=[s_pa[pq]])
                b.rw("tensor", lambda e, p=p, o=o, pq=pq: e.matmul(
                    out=pa[pq][:], lhsT=ones_col[:],
                    rhs=w_small["f0_bout"][0:1, o * 512:(o + 1) * 512], start=False, stop=True),
                    reads=[s_ones, s_small["f0_bout"]], writes=[s_pa[pq]])
                b.rw("vector", lambda v, t=t, o=o, pq=pq: v.scalar_tensor_tensor(
                    out=stream[:, t, o * 512:(o + 1) * 512], in0=pa[pq][:],
                    scalar=0.5, in1=stream[:, t, o * 512:(o + 1) * 512],
                    op0=ALU.mult, op1=ALU.add),
                    reads=[s_pa[pq], s_stream[t]], writes=[s_stream[t]])
        b.barrier()
        fs.close()

    def emit_l0attn():
        H, L, NSEG = 8, 512, T // 512
        fs = ExitStack()
        fsb = lambda name, shape, dt=BF16: fs.enter_context(nc.sbuf_tensor(name, shape, dt, side="right"))
        wq_sb = fsb("a_wq", [128, NC_DC, D]); s_wq = Buf()
        wk_sb = fsb("a_wk", [128, NC_DC, D]); s_wk = Buf()
        wv_sb = fsb("a_wv", [128, NC_DC, D]); s_wv = Buf()
        qf = fsb("a_qf", [128, H, L]); s_qf = Buf()
        kf = fsb("a_kf", [128, H, L]); s_kf = Buf()
        sqf = fsb("a_sqf", [128, H, L]); s_sqf = Buf()
        skf = fsb("a_skf", [128, H, L]); s_skf = Buf()
        xfo = fsb("a_xfo", [128, 4096])
        s_xfo = Buf()
        vt = fsb("a_vt", [128, 4, 1024]); s_vt = Buf()
        vd = fsb("a_vd", [128, 4, 129])
        s_vd = [Buf() for _ in range(4)]
        s_vd1 = Buf()
        m32 = fsb("a_m32", [128, H, 129], F32); s_m32 = Buf()
        mbf = fsb("a_mbf", [128, H, 129]); s_mbf = Buf()
        esb = fsb("a_esb", [128, 4, L])
        s_esb = [Buf() for _ in range(4)]
        etb = [fsb("a_etb0", [128, 4, 128])]
        s_etb = [Buf()]
        skt = [fsb("a_skt0", [128, 128])]
        s_skt = [Buf()]
        amg = fsb("a_amg", [128, 4, 128])
        s_amg = [Buf() for _ in range(4)]
        esbf = esb[:].rearrange("p a l -> p (a l)")
        axh = [esbf[:, 0:D], esbf[:, D:2 * D]]

        sm = fsb("a_sm", [128, 4, 16], F32)
        s_sm = [Buf() for _ in range(4)]

        qv = qf[:].rearrange("p h l -> p (h l)")
        kv = kf[:].rearrange("p h l -> p (h l)")
        emit_ln_stats([qv[:, 0:D], qv[:, D:2 * D]], [s_qf, s_qf],
                      [kv[:, 0:D], kv[:, D:2 * D]], [s_kf, s_kf])

        for wname, wsb, s_w in [("l0_wq", wq_sb, s_wq), ("l0_wk", wk_sb, s_wk), ("l0_wv", wv_sb, s_wv)]:
            for c in range(NC_DC):
                b.dma_rw("sync", stage32[:], dw[wname][c * 128:(c + 1) * 128, :], writes=[s_stg[0]])
                b.rw("vector", lambda v, c=c, wsb=wsb: v.tensor_copy(
                    out=wsb[:, c], in_=stage32[:]),
                    reads=[s_stg[0]], writes=[s_w])

        b.rw("gpsimd", lambda g: g.memset(m32[:], 0.0), writes=[s_m32])
        b.rw("vector", lambda v: v.tensor_copy(out=mbf[:], in_=m32[:]), reads=[s_m32], writes=[s_mbf])
        b.rw("gpsimd", lambda g: g.memset(vd[:, :, 128:129], 1.0), writes=[s_vd1])

        INV = 1.0 / float(np.sqrt(np.float32(128.0)))
        for s in range(NSEG):
            xf = xfo[:].rearrange("p (c l) -> p c l", l=L)
            aout = xfo[:].rearrange("p (lt e) -> p lt e", e=1024)
            for i in range(4):
                t = 4 * s + i
                p = i % 2
                b.rw("scalar", lambda sc, t=t, p=p: sc.activation(
                    out=axh[p], in_=stream[:, t], func=AF.Identity,
                    bias=st_bias[:, t:t + 1], scale=st_rstd[:, t:t + 1]),
                    reads=[s_stream[t], s_bias, s_rstd],
                    writes=[s_esb[2 * p], s_esb[2 * p + 1]])
                for c in range(NC_DC):
                    pc = c % 2
                    b.rw("tensor", lambda e, c=c, p=p, pc=pc: e.transpose(
                        out=pt[pc][:], in_=axh[p][:, c * 128:(c + 1) * 128], identity=ident[:]),
                        reads=[s_esb[2 * p], s_esb[2 * p + 1], s_ident], writes=[s_pt[pc]])
                    b.rw("scalar", lambda sc, c=c, i=i, pc=pc: sc.copy(
                        out=xf[:, c, i * 128:(i + 1) * 128], in_=pt[pc][:]),
                        reads=[s_pt[pc]], writes=[s_xfo])
            for dst, s_dst, wsb, s_w, brow in ((qf, s_qf, wq_sb, s_wq, "l0_bq"), (kf, s_kf, wk_sb, s_wk, "l0_bk")):
                for j in range(H):
                    pq = j % 2
                    for c in range(NC_DC):
                        b.rw("tensor", lambda e, j=j, c=c, pq=pq, wsb=wsb: e.matmul(
                            out=pa[pq][:], lhsT=wsb[:, c, j * 128:(j + 1) * 128],
                            rhs=xf[:, c, :], start=(c == 0), stop=False),
                            reads=[s_w, s_xfo], writes=[s_pa[pq]])
                    b.rw("tensor", lambda e, j=j, pq=pq, brow=brow: e.matmul(
                        out=pa[pq][:], lhsT=w_small[brow][0:1, j * 128:(j + 1) * 128],
                        rhs=ones_row[:], start=False, stop=True),
                        reads=[s_small[brow], s_onesr], writes=[s_pa[pq]])
                    b.rw("vector", lambda v, j=j, pq=pq, dst=dst: v.tensor_copy(
                        out=dst[:, j], in_=pa[pq][:]), reads=[s_pa[pq]], writes=[s_dst])
            for src_t, s_src, dst_t, s_dst in ((qf, s_qf, sqf, s_sqf), (kf, s_kf, skf, s_skf)):
                for j in range(H):
                    pe = j % 4
                    b.rw("vector", lambda v, j=j, pe=pe, src_t=src_t: v.tensor_single_scalar(
                        out=esb[:, pe, :], in_=src_t[:, j], scalar=0.0, op=ALU.min),
                        reads=[s_src], writes=[s_esb[pe]])
                    b.rw("scalar", lambda sc, j=j, pe=pe: sc.activation(
                        out=esb[:, pe, :], in_=esb[:, pe, :], func=AF.Exp),
                        reads=[s_esb[pe]], writes=[s_esb[pe]])
                    b.rw("vector", lambda v, j=j, pe=pe, src_t=src_t, dst_t=dst_t: v.scalar_tensor_tensor(
                        out=dst_t[:, j], in0=src_t[:, j], scalar=0.0, in1=esb[:, pe, :],
                        op0=ALU.max, op1=ALU.add),
                        reads=[s_src, s_esb[pe]], writes=[s_dst])
            for lt in range(4):
                for half in range(2):
                    pq = (2 * lt + half) % 2 + 2
                    for c in range(NC_DC):
                        b.rw("tensor", lambda e, lt=lt, half=half, c=c, pq=pq: e.matmul(
                            out=pa[pq][:], lhsT=xf[:, c, lt * 128:(lt + 1) * 128],
                            rhs=wv_sb[:, c, half * 512:(half + 1) * 512], start=(c == 0), stop=False),
                            reads=[s_xfo, s_wv], writes=[s_pa[pq]])
                    b.rw("tensor", lambda e, half=half, pq=pq: e.matmul(
                        out=pa[pq][:], lhsT=ones_col[:],
                        rhs=w_small["l0_bv"][0:1, half * 512:(half + 1) * 512], start=False, stop=True),
                        reads=[s_ones, s_small["l0_bv"]], writes=[s_pa[pq]])
                    b.rw("scalar", lambda sc, lt=lt, half=half, pq=pq: sc.copy(
                        out=vt[:, lt, half * 512:(half + 1) * 512], in_=pa[pq][:]),
                        reads=[s_pa[pq]], writes=[s_vt])
            for h in range(H):
                for ltpair in ((0, 1), (2, 3)):
                  for lt in ltpair:
                      w = 128 * (lt + 1)
                      pS = lt % 2
                      smv = sm[:, lt, :]
                      b.rw("tensor", lambda e, h=h, lt=lt, w=w, pS=pS: e.matmul(
                          out=pa[pS][:, 0:w], lhsT=qf[:, h, lt * 128:(lt + 1) * 128],
                          rhs=kf[:, h, 0:w], start=True, stop=True),
                          reads=[s_qf, s_kf], writes=[s_pa[pS]])
                      b.rw("vector", lambda v, lt=lt, w=w, pS=pS, smv=smv: v.reduce_max(
                          out=smv[:, 0:1], in_=pa[pS][:, 0:w], axis=AX.X),
                          reads=[s_pa[pS]], writes=[s_sm[lt]])
                      b.rw("vector", lambda v, smv=smv: v.tensor_scalar(
                          out=smv[:, 1:2], in0=smv[:, 0:1], scalar1=-INV, scalar2=0.0,
                          op0=ALU.mult, op1=ALU.add),
                          reads=[s_sm[lt]], writes=[s_sm[lt]])
                  for lt in ltpair:
                      w = 128 * (lt + 1)
                      pS = lt % 2
                      smv = sm[:, lt, :]
                      if lt > 0:
                          b.rw("scalar", lambda sc, lt=lt, pS=pS, smv=smv: sc.activation(
                              out=esb[:, lt, 0:128 * lt], in_=pa[pS][:, 0:128 * lt], func=AF.Exp,
                              bias=smv[:, 1:2], scale=INV, accum_out=smv[:, 2:3]),
                              reads=[s_pa[pS], s_sm[lt]], writes=[s_esb[lt], s_sm[lt]])
                      b.rw("scalar", lambda sc, lt=lt, w=w, pS=pS, smv=smv: sc.activation(
                          out=esb[:, lt, 128 * lt:w], in_=pa[pS][:, 128 * lt:w], func=AF.Exp,
                          bias=smv[:, 1:2], scale=INV),
                          reads=[s_pa[pS], s_sm[lt]], writes=[s_esb[lt]])
                for lt in range(4):
                    smv = sm[:, lt, :]
                    pQ = lt % 2
                    b.rw("tensor", lambda e, h=h, lt=lt, pQ=pQ: e.matmul(
                        out=pa[pQ][:, 0:129], lhsT=sqf[:, h, lt * 128:(lt + 1) * 128],
                        rhs=mbf[:, h], start=True, stop=True),
                        reads=[s_sqf, s_mbf], writes=[s_pa[pQ]])
                    b.rw("vector", lambda v, pQ=pQ, smv=smv: v.tensor_scalar(
                        out=smv[:, 6:7], in0=pa[pQ][:, 128:129], scalar1=EPS_ATT, scalar2=0.0,
                        op0=ALU.add, op1=ALU.add),
                        reads=[s_pa[pQ]], writes=[s_sm[lt]])
                    b.rw("vector", lambda v, smv=smv: v.reciprocal(out=smv[:, 7:8], in_=smv[:, 6:7]),
                        reads=[s_sm[lt]], writes=[s_sm[lt]])
                    b.rw("vector", lambda v, h=h, smv=smv: v.tensor_tensor(
                        out=smv[:, 8:9], in0=smv[:, 7:8], in1=g8[:, h:h + 1], op=ALU.mult),
                        reads=[s_sm[lt], s_g8], writes=[s_sm[lt]])
                    b.rw("scalar", lambda sc, lt=lt, pQ=pQ, smv=smv: sc.activation(
                        out=amg[:, lt, :], in_=pa[pQ][:, 0:128], func=AF.Identity, scale=smv[:, 8:9]),
                        reads=[s_pa[pQ], s_sm[lt]], writes=[s_amg[lt]])
                for lt in range(4):
                    smv = sm[:, lt, :]
                    pK = 2 if lt % 2 == 0 else 3
                    b.rw("tensor", lambda e, h=h, lt=lt, pK=pK: e.matmul(
                        out=pa[pK][:, 0:129], lhsT=skf[:, h, lt * 128:(lt + 1) * 128],
                        rhs=mbf[:, h], start=True, stop=True),
                        reads=[s_skf, s_mbf], writes=[s_pa[pK]])
                    b.rw("vector", lambda v, pK=pK, smv=smv: v.tensor_scalar(
                        out=smv[:, 9:10], in0=pa[pK][:, 128:129], scalar1=EPS_ATT, scalar2=0.0,
                        op0=ALU.add, op1=ALU.add),
                        reads=[s_pa[pK]], writes=[s_sm[lt]])
                    b.rw("vector", lambda v, smv=smv: v.reciprocal(out=smv[:, 10:11], in_=smv[:, 9:10]),
                        reads=[s_sm[lt]], writes=[s_sm[lt]])
                    b.rw("vector", lambda v, smv=smv: v.tensor_scalar(
                        out=smv[:, 11:12], in0=smv[:, 10:11], scalar1=-1.0, scalar2=0.0,
                        op0=ALU.mult, op1=ALU.add),
                        reads=[s_sm[lt]], writes=[s_sm[lt]])
                    b.rw("vector", lambda v, h=h, lt=lt, pK=pK, smv=smv: v.scalar_tensor_tensor(
                        out=vd[:, lt, 0:128], in0=pa[pK][:, 0:128], scalar=smv[:, 11:12],
                        in1=vt[:, lt, h * 128:(h + 1) * 128], op0=ALU.mult, op1=ALU.add),
                        reads=[s_pa[pK], s_sm[lt], s_vt], writes=[s_vd[lt]])
                    b.rw("tensor", lambda e, h=h, lt=lt: e.transpose(
                        out=pt[0][:], in_=skf[:, h, lt * 128:(lt + 1) * 128], identity=ident[:]),
                        reads=[s_skf, s_ident], writes=[s_pt[0]])
                    b.rw("scalar", lambda sc: sc.copy(out=skt[0][:], in_=pt[0][:]),
                        reads=[s_pt[0]], writes=[s_skt[0]])
                    b.rw("tensor", lambda e, h=h, lt=lt: e.matmul(
                        out=pmu[:, 0:129], lhsT=skt[0][:], rhs=vd[:, lt, :],
                        start=(lt == 0), stop=(lt == 3)),
                        reads=[s_skt[0], s_vd[lt], s_vd1], writes=[s_pmu])
                for lt in range(4):
                    w = 128 * (lt + 1)
                    smv = sm[:, lt, :]
                    b.rw("vector", lambda v, lt=lt, w=w, smv=smv: v.scalar_tensor_tensor(
                        out=esb[:, lt, 128 * lt:w], in0=esb[:, lt, 128 * lt:w], scalar=1.0,
                        in1=lmask[:], op0=ALU.mult, op1=ALU.mult, accum_out=smv[:, 3:4]),
                        reads=[s_esb[lt], s_lmask], writes=[s_esb[lt], s_sm[lt]])
                    if lt > 0:
                        b.rw("vector", lambda v, smv=smv: v.tensor_tensor(
                            out=smv[:, 3:4], in0=smv[:, 3:4], in1=smv[:, 2:3], op=ALU.add),
                            reads=[s_sm[lt]], writes=[s_sm[lt]])
                    b.rw("vector", lambda v, smv=smv: v.reciprocal(out=smv[:, 4:5], in_=smv[:, 3:4]),
                        reads=[s_sm[lt]], writes=[s_sm[lt]])
                    b.rw("vector", lambda v, h=h, smv=smv: v.tensor_tensor(
                        out=smv[:, 5:6], in0=smv[:, 4:5], in1=omg8[:, h:h + 1], op=ALU.mult),
                        reads=[s_sm[lt], s_omg8], writes=[s_sm[lt]])
                for lt in range(4):
                    smv = sm[:, lt, :]
                    pA = 2 + lt % 2
                    for mt in range(lt + 1):
                        pc = mt % 2
                        b.rw("tensor", lambda e, lt=lt, mt=mt, pc=pc: e.transpose(
                            out=pt[pc][:], in_=esb[:, lt, mt * 128:(mt + 1) * 128], identity=ident[:]),
                            reads=[s_esb[lt], s_ident], writes=[s_pt[pc]])
                        b.rw("scalar", lambda sc, mt=mt, pc=pc: sc.copy(
                            out=etb[0][:, mt], in_=pt[pc][:]),
                            reads=[s_pt[pc]], writes=[s_etb[0]])
                    for mt in range(lt + 1):
                        b.rw("tensor", lambda e, h=h, mt=mt, lt=lt, pA=pA: e.matmul(
                            out=pa[pA][:, 0:128], lhsT=etb[0][:, mt],
                            rhs=vt[:, mt, h * 128:(h + 1) * 128], start=(mt == 0), stop=(mt == lt)),
                            reads=[s_etb[0], s_vt], writes=[s_pa[pA]])
                    b.rw("vector", lambda v, h=h, lt=lt, pA=pA, smv=smv: v.scalar_tensor_tensor(
                        out=aout[:, lt, h * 128:(h + 1) * 128], in0=pa[pA][:, 0:128],
                        scalar=smv[:, 5:6], in1=amg[:, lt, :], op0=ALU.mult, op1=ALU.add),
                        reads=[s_pa[pA], s_sm[lt], s_amg[lt]], writes=[s_xfo])
                b.rw("vector", lambda v, h=h: v.tensor_tensor(
                    out=m32[:, h], in0=pmu[:, 0:129], in1=m32[:, h], op=ALU.add),
                    reads=[s_pmu, s_m32], writes=[s_m32])
                b.rw("vector", lambda v, h=h: v.tensor_copy(out=mbf[:, h], in_=m32[:, h]),
                    reads=[s_m32], writes=[s_mbf])
            for lt in range(4):
                t = 4 * s + lt
                b.rw("vector", lambda v, t=t, lt=lt: v.tensor_tensor(
                    out=stream[:, t], in0=aout[:, lt], in1=stream[:, t], op=ALU.add),
                    reads=[s_xfo, s_stream[t]], writes=[s_stream[t]])
        b.barrier()
        fs.close()

    GU_CH = [512, 512, 512, 512, 512, 256]
    GU_OFF = [0, 512, 1024, 1536, 2048, 2560]

    def emit_wquant():
        fs = ExitStack()
        fsb = lambda name, shape, dt=F32: fs.enter_context(nc.sbuf_tensor(name, shape, dt, side="right"))
        wst = [fsb(f"wq_st{i}", [128, I_FFN]) for i in range(2)]
        s_wst = [Buf(), Buf()]
        v32 = fsb("wq_v32", [128, I_FFN]); s_v32 = Buf()
        u32 = fsb("wq_u32", [128, I_FFN]); s_u32 = Buf()
        qb = fsb("wq_qb", [128, I_FFN], BF16); s_qb = Buf()
        qm = fsb("wq_qm", [128, I_FFN], BF16); s_qm = Buf()
        mk = fsb("wq_mk", [128, I_FFN], U8); s_mk = Buf()
        wqt = fsb("wq_wqt", [128, I_FFN], BF16); s_wqt = Buf()
        mx = fsb("wq_mx", [128, 176]); s_mx = Buf()
        scl = fsb("wq_scl", [128, 176]); s_scl = Buf()
        rscl = fsb("wq_rscl", [128, 176]); s_rscl = Buf()
        tst = [fsb(f"wq_tst{i}", [128, 128], BF16) for i in range(2)]
        s_tst = [Buf(), Buf()]

        def quant_tile(width, s_src, src_ap):
            nblk = width // 16
            b.rw("vector", lambda v: v.tensor_reduce(
                out=mx[:, 0:nblk], in_=src_ap.rearrange("p (n k) -> p n k", k=16),
                axis=AX.X, op=ALU.max, apply_absolute_value=True),
                reads=[s_src], writes=[s_mx])
            b.rw("vector", lambda v: v.tensor_scalar(
                out=scl[:, 0:nblk], in0=mx[:, 0:nblk], scalar1=1.0 / 6.0, scalar2=1e-12,
                op0=ALU.mult, op1=ALU.max), reads=[s_mx], writes=[s_scl])
            b.rw("vector", lambda v: v.reciprocal(out=rscl[:, 0:nblk], in_=scl[:, 0:nblk]),
                reads=[s_scl], writes=[s_rscl])
            rb = rscl[:, 0:nblk].rearrange("p (n o) -> p n o", o=1).broadcast_to([128, nblk, 16])
            b.rw("vector", lambda v, rb=rb: v.tensor_tensor(
                out=v32[:, 0:width].rearrange("p (n k) -> p n k", k=16), in0=rb,
                in1=src_ap.rearrange("p (n k) -> p n k", k=16), op=ALU.mult),
                reads=[s_rscl, s_src], writes=[s_v32])
            b.rw("scalar", lambda s: s.activation(out=u32[:, 0:width], in_=v32[:, 0:width], func=AF.Abs),
                reads=[s_v32], writes=[s_u32])
            b.rw("vector", lambda v: v.tensor_scalar(
                out=qb[:, 0:width], in0=u32[:, 0:width], scalar1=QMID[0], scalar2=QINC[0],
                op0=ALU.is_gt, op1=ALU.mult), reads=[s_u32], writes=[s_qb])
            for i in range(1, 7):
                b.rw("vector", lambda v, i=i: v.tensor_scalar(
                    out=qm[:, 0:width], in0=u32[:, 0:width], scalar1=QMID[i], scalar2=QINC[i],
                    op0=ALU.is_gt, op1=ALU.mult), reads=[s_u32], writes=[s_qm])
                b.rw("vector", lambda v: v.tensor_tensor(
                    out=qb[:, 0:width], in0=qb[:, 0:width], in1=qm[:, 0:width], op=ALU.add),
                    reads=[s_qb, s_qm], writes=[s_qb])
            b.rw("vector", lambda v: v.tensor_single_scalar(
                out=mk[:, 0:width], in_=v32[:, 0:width], scalar=0.0, op=ALU.is_lt),
                reads=[s_v32], writes=[s_mk])
            b.rw("vector", lambda v: v.tensor_scalar(
                out=qm[:, 0:width], in0=qb[:, 0:width], scalar1=-1.0, scalar2=0.0,
                op0=ALU.mult, op1=ALU.add), reads=[s_qb], writes=[s_qm])
            b.rw("vector", lambda v: v.copy_predicated(
                out=qb[:, 0:width], mask=mk[:, 0:width], data=qm[:, 0:width]),
                reads=[s_mk, s_qm], writes=[s_qb])
            sbx = scl[:, 0:nblk].rearrange("p (n o) -> p n o", o=1).broadcast_to([128, nblk, 16])
            b.rw("vector", lambda v, sbx=sbx: v.tensor_tensor(
                out=wqt[:, 0:width].rearrange("p (n k) -> p n k", k=16), in0=sbx,
                in1=qb[:, 0:width].rearrange("p (n k) -> p n k", k=16), op=ALU.mult),
                reads=[s_scl, s_qb], writes=[s_wqt])

        for wname, ddst in (("f1_wg", d_wgq), ("f1_wu", d_wuq)):
            dflat = ddst.rearrange("(p c i) -> p c i", c=NC_DC, i=I_FFN)
            for it in range(22):
                pw = it % 2
                b.dma_rw("sync", wst[pw][:, 0:D], dw[wname][it * 128:(it + 1) * 128, :], writes=[s_wst[pw]])
                quant_tile(D, s_wst[pw], wst[pw][:, 0:D])
                for c in range(NC_DC):
                    pc = c % 2
                    b.rw("tensor", lambda e, c=c, pc=pc: e.transpose(
                        out=pt[pc][:], in_=wqt[:, c * 128:(c + 1) * 128], identity=ident[:]),
                        reads=[s_wqt, s_ident], writes=[s_pt[pc]])
                    b.rw("scalar", lambda s, c=c, pc=pc: s.copy(out=tst[pc][:], in_=pt[pc][:]),
                        reads=[s_pt[pc]], writes=[s_tst[pc]])
                    b.dma_rw("gpsimd", dflat[:, c, it * 128:(it + 1) * 128], tst[pc][:],
                             reads=[s_tst[pc]])
        for ot in range(NC_DC):
            pw = ot % 2
            b.dma_rw("sync", wst[pw][:], dw["f1_wd"][ot * 128:(ot + 1) * 128, :], writes=[s_wst[pw]])
            quant_tile(I_FFN, s_wst[pw], wst[pw][:])
            for cc in range(22):
                pc = cc % 2
                b.rw("tensor", lambda e, cc=cc, pc=pc: e.transpose(
                    out=pt[pc][:], in_=wqt[:, cc * 128:(cc + 1) * 128], identity=ident[:]),
                    reads=[s_wqt, s_ident], writes=[s_pt[pc]])
                b.rw("scalar", lambda s, cc=cc, pc=pc: s.copy(out=tst[pc][:], in_=pt[pc][:]),
                    reads=[s_pt[pc]], writes=[s_tst[pc]])
                b.dma_rw("gpsimd", d_wdq[cc, :, ot * 128:(ot + 1) * 128], tst[pc][:],
                         reads=[s_tst[pc]])
        b.barrier()
        fs.close()

    _f1_call = [0]

    def emit_ffn1():
        _f1_call[0] += 1
        sfx = f"_q{_f1_call[0]}"
        fs = ExitStack()
        fsb = lambda name, shape, dt=BF16: fs.enter_context(nc.sbuf_tensor(name + sfx, shape, dt, side="right"))
        wg_ch = [fsb(f"f1_wg{i}", [128, NC_DC, 512]) for i in range(2)]
        wu_ch = [fsb(f"f1_wu{i}", [128, NC_DC, 512]) for i in range(2)]
        hqT = fsb("f1_hqT", [128, 4, 22, 128]); s_hqT = Buf()
        hqs = fsb("f1_hqs", [128, 4, 512]); s_hqs = Buf()
        NWDR = 2
        wdr = fsb("f1_wdr", [128, NWDR, 512])
        s_wdr = [[Buf(), Buf()] for _ in range(NWDR)]
        s_wg = [Buf(), Buf()]; s_wu = [Buf(), Buf()]
        xq_fm = fsb("f1_xqfm", [128, NC_DC, 512]); s_xqfm = Buf()
        xh32 = fsb("f1_xh", [128, 2, D], F32); s_xh = Buf()
        xqs = [fsb(f"f1_xqs{i}", [128, 2, D]) for i in range(2)]
        s_xqs = [Buf(), Buf()]
        h32 = fsb("f1_h32", [128, 4, 512], F32); s_h32 = Buf()
        v16 = fsb("f1_v16", [128, 2048], FP16); s_v16 = Buf()
        u16 = fsb("f1_u16", [128, 2048], FP16); s_u16 = Buf()
        qb6 = fsb("f1_qb", [128, 2048]); s_qb6 = Buf()
        qm6 = fsb("f1_qm", [128, 2048]); s_qm6 = Buf()
        mk6 = fsb("f1_mk", [128, 2048], U8); s_mk6 = Buf()
        mxs = fsb("f1_mxs", [128, 128], F32); s_mxs = Buf()
        scls = fsb("f1_scls", [128, 128], F32); s_scls = Buf()
        rscls = fsb("f1_rscls", [128, 128], F32); s_rscls = Buf()

        def quant_batch(src4, s_src, dst4, s_dst, A, N):
            nblk = A * N
            W = nblk * 16
            b.rw("vector", lambda v: v.tensor_reduce(
                out=mxs[:, 0:nblk].rearrange("p (a n) -> p a n", a=A), in_=src4,
                axis=AX.X, op=ALU.max,
                apply_absolute_value=True), reads=list(s_src), writes=[s_mxs])
            b.rw("vector", lambda v: v.tensor_scalar(
                out=scls[:, 0:nblk], in0=mxs[:, 0:nblk], scalar1=1.0 / 6.0, scalar2=1e-12,
                op0=ALU.mult, op1=ALU.max), reads=[s_mxs], writes=[s_scls])
            b.rw("vector", lambda v: v.reciprocal(out=rscls[:, 0:nblk], in_=scls[:, 0:nblk]),
                reads=[s_scls], writes=[s_rscls])
            rb = rscls[:, 0:nblk].rearrange("p (a n o) -> p a n o", a=A, o=1).broadcast_to([128, A, N, 16])
            vv = v16[:, 0:W].rearrange("p (a n k) -> p a n k", a=A, k=16)
            b.rw("vector", lambda v, rb=rb, vv=vv: v.tensor_tensor(
                out=vv, in0=rb, in1=src4, op=ALU.mult),
                reads=[s_rscls] + list(s_src), writes=[s_v16])
            b.rw("scalar", lambda s: s.activation(
                out=u16[:, 0:W], in_=v16[:, 0:W], func=AF.Abs),
                reads=[s_v16], writes=[s_u16])
            b.rw("vector", lambda v: v.tensor_scalar(
                out=qb6[:, 0:W], in0=u16[:, 0:W], scalar1=QMID[0], scalar2=QINC[0],
                op0=ALU.is_gt, op1=ALU.mult), reads=[s_u16], writes=[s_qb6])
            for i in range(1, 7):
                b.rw("vector", lambda v, i=i: v.tensor_scalar(
                    out=qm6[:, 0:W], in0=u16[:, 0:W], scalar1=QMID[i], scalar2=QINC[i],
                    op0=ALU.is_gt, op1=ALU.mult), reads=[s_u16], writes=[s_qm6])
                b.rw("vector", lambda v: v.tensor_tensor(
                    out=qb6[:, 0:W], in0=qb6[:, 0:W], in1=qm6[:, 0:W], op=ALU.add),
                    reads=[s_qb6, s_qm6], writes=[s_qb6])
            b.rw("vector", lambda v: v.tensor_single_scalar(
                out=mk6[:, 0:W], in_=v16[:, 0:W], scalar=0.0, op=ALU.is_lt),
                reads=[s_v16], writes=[s_mk6])
            b.rw("vector", lambda v: v.tensor_scalar(
                out=qm6[:, 0:W], in0=qb6[:, 0:W], scalar1=-1.0, scalar2=0.0,
                op0=ALU.mult, op1=ALU.add), reads=[s_qb6], writes=[s_qm6])
            b.rw("vector", lambda v: v.copy_predicated(
                out=qb6[:, 0:W], mask=mk6[:, 0:W], data=qm6[:, 0:W]),
                reads=[s_mk6, s_qm6], writes=[s_qb6])
            sbx = scls[:, 0:nblk].rearrange("p (a n o) -> p a n o", a=A, o=1).broadcast_to([128, A, N, 16])
            qv = qb6[:, 0:W].rearrange("p (a n k) -> p a n k", a=A, k=16)
            b.rw("vector", lambda v, sbx=sbx, qv=qv: v.tensor_tensor(
                out=dst4, in0=sbx, in1=qv, op=ALU.mult),
                reads=[s_scls, s_qb6], writes=list(s_dst))

        dgq = d_wgq.rearrange("(p c i) -> p c i", c=NC_DC, i=I_FFN)
        duq = d_wuq.rearrange("(p c i) -> p c i", c=NC_DC, i=I_FFN)
        emit_ln_stats([qb6[:, 0:D], qb6[:, D:2 * D]], [s_qb6, s_qb6],
                      [qm6[:, 0:D], qm6[:, D:2 * D]], [s_qm6, s_qm6])
        xq_st = [None]

        def emit_xquant(st):
            for half in range(2):
                for i2 in range(2):
                    t = 4 * st + 2 * half + i2
                    emit_xhat(t, xh32[:, i2, :], s_xh)
                quant_batch(xh32[:].rearrange("p a (n k) -> p a n k", k=16), [s_xh],
                            xqs[half][:].rearrange("p a (n k) -> p a n k", k=16), [s_xqs[half]], 2, 64)

        def emit_xtrans(st):
            for half in range(2):
                for i2 in range(2):
                    i = 2 * half + i2
                    for c in range(NC_DC):
                        pc = c % 2
                        b.rw("tensor", lambda e, c=c, i2=i2, half=half, pc=pc: e.transpose(
                            out=pt[pc][:], in_=xqs[half][:, i2, c * 128:(c + 1) * 128], identity=ident[:]),
                            reads=[s_xqs[half], s_ident], writes=[s_pt[pc]])
                        b.rw("scalar", lambda s, c=c, i=i, pc=pc: s.copy(
                            out=xq_fm[:, c, i * 128:(i + 1) * 128], in_=pt[pc][:]),
                            reads=[s_pt[pc]], writes=[s_xqfm])

        emit_xquant(0)
        for st in range(8):
            emit_xtrans(st)
            pend_T = []
            for ic in range(6):
                w = GU_CH[ic]; i0 = GU_OFF[ic]
                pw = ic % 2
                b.dma_rw("sync", wg_ch[pw][:, :, 0:w], dgq[:, :, i0:i0 + w], writes=[s_wg[pw]])
                b.dma_rw("sync", wu_ch[pw][:, :, 0:w], duq[:, :, i0:i0 + w], writes=[s_wu[pw]])
                for tt in range(4):
                    ph = 0
                    pg, pu = 2 * (tt % 2), 2 * (tt % 2) + 1
                    for c in range(NC_DC):
                        b.rw("tensor", lambda e, c=c, tt=tt, w=w, pw=pw, pg=pg: e.matmul(
                            out=pa[pg][:, 0:w], lhsT=xq_fm[:, c, tt * 128:(tt + 1) * 128],
                            rhs=wg_ch[pw][:, c, 0:w], start=(c == 0), stop=(c == NC_DC - 1)),
                            reads=[s_xqfm, s_wg[pw]], writes=[s_pa[pg]])
                    for c in range(NC_DC):
                        b.rw("tensor", lambda e, c=c, tt=tt, w=w, pw=pw, pu=pu: e.matmul(
                            out=pa[pu][:, 0:w], lhsT=xq_fm[:, c, tt * 128:(tt + 1) * 128],
                            rhs=wu_ch[pw][:, c, 0:w], start=(c == 0), stop=(c == NC_DC - 1)),
                            reads=[s_xqfm, s_wu[pw]], writes=[s_pa[pu]])
                    b.rw("scalar", lambda s, w=w, tt=tt, pg=pg: s.activation(
                        out=h32[:, tt, 0:w], in_=pa[pg][:, 0:w], func=AF.Silu),
                        reads=[s_pa[pg]], writes=[s_h32])
                    b.rw("vector", lambda v, w=w, tt=tt, pu=pu: v.tensor_tensor(
                        out=h32[:, tt, 0:w],
                        in0=h32[:, tt, 0:w], in1=pa[pu][:, 0:w], op=ALU.mult),
                        reads=[s_h32, s_pa[pu]], writes=[s_h32])
                for f in pend_T:
                    f()
                pend_T = []
                quant_batch(h32[:, :, 0:w].rearrange("p a (n k) -> p a n k", k=16), [s_h32],
                            hqs[:, :, 0:w].rearrange("p a (n k) -> p a n k", k=16),
                            [s_hqs], 4, w // 16)
                cc0 = i0 // 128
                for tt in range(4):
                    for j in range(w // 128):
                        pc = (tt * (w // 128) + j) % 2
                        def emit_T(tt=tt, j=j, pc=pc, cc=cc0 + j):
                            b.rw("tensor", lambda e, tt=tt, j=j, pc=pc: e.transpose(
                                out=pt[pc][:], in_=hqs[:, tt, j * 128:(j + 1) * 128], identity=ident[:]),
                                reads=[s_hqs, s_ident], writes=[s_pt[pc]])
                            b.rw("scalar", lambda s, tt=tt, cc=cc, pc=pc: s.copy(
                                out=hqT[:, tt, cc], in_=pt[pc][:]),
                                reads=[s_pt[pc]], writes=[s_hqT])
                        pend_T.append(emit_T)
            for f in pend_T:
                f()
            if st + 1 < 8:
                emit_xquant(st + 1)
            for o in range(2):
                def wd_dma(cc, o=o):
                    b.dma_rw("gpsimd", wdr[:, cc % NWDR, 0:256],
                             d_wdq[cc, :, o * 512:o * 512 + 256],
                             writes=[s_wdr[cc % NWDR][0]])
                    b.dma_rw("scalar", wdr[:, cc % NWDR, 256:512],
                             d_wdq[cc, :, o * 512 + 256:o * 512 + 512],
                             writes=[s_wdr[cc % NWDR][1]])
                for cc in range(NWDR):
                    wd_dma(cc)
                for cc in range(22):
                    sl = cc % NWDR
                    for tt in range(4):
                        b.rw("tensor", lambda e, cc=cc, tt=tt, sl=sl: e.matmul(
                            out=pa[tt][:], lhsT=hqT[:, tt, cc], rhs=wdr[:, sl],
                            start=(cc == 0), stop=(cc == 21)),
                            reads=[s_hqT, s_wdr[sl][0], s_wdr[sl][1]], writes=[s_pa[tt]])
                    if cc + NWDR < 22:
                        wd_dma(cc + NWDR)
                for tt in range(4):
                    t = 4 * st + tt
                    b.rw("vector", lambda v, t=t, tt=tt, o=o: v.scalar_tensor_tensor(
                        out=stream[:, t, o * 512:(o + 1) * 512], in0=pa[tt][:],
                        scalar=0.5, in1=stream[:, t, o * 512:(o + 1) * 512],
                        op0=ALU.mult, op1=ALU.add),
                        reads=[s_pa[tt], s_stream[t]], writes=[s_stream[t]])
        b.barrier()
        fs.close()

    def emit_l1attn():
        H1, HD = 16, 64
        fs = ExitStack()
        fsb = lambda name, shape, dt=BF16: fs.enter_context(nc.sbuf_tensor(name, shape, dt, side="right"))
        w2 = fsb("l1_w2", [128, NC_DC, 2048]); s_w2 = Buf()
        kvE = fsb("l1_kvE", [128, 8, 65], F32); s_kvE = Buf()
        kvO = fsb("l1_kvO", [128, 8, 65], F32); s_kvO = Buf()
        kv2b = fsb("l1_kv2b", [128, 8, 130]); s_kv2b = Buf()
        axh = [fsb(f"l1_xh{i}", [128, D]) for i in range(2)]
        s_axh = [Buf(), Buf()]
        xf_t = [fsb(f"l1_xf{i}", [128, NC_DC, 128]) for i in range(2)]
        s_xft = [Buf(), Buf()]
        fk_tm = [fsb(f"l1_fk{i}", [128, D]) for i in range(2)]
        s_fk = [Buf(), Buf()]
        v_tm = [fsb(f"l1_v{i}", [128, 8, 130]) for i in range(2)]
        s_v = [Buf(), Buf()]
        s_vones = [Buf(), Buf()]
        scr32 = [fsb(f"l1_scr{i}", [128, 512], F32) for i in range(2)]
        s_scr = [Buf(), Buf()]
        xf_g = fsb("l1_xfg", [128, NC_DC, 512]); s_xfg = Buf()
        fq_g = fsb("l1_fqg", [128, NC_DC, 512]); s_fqg = Buf()
        out_tm = fsb("l1_out", [128, 4, D]); s_out = Buf()
        ofm = [fsb(f"l1_ofm{i}", [128, NC_DC, 128]) for i in range(2)]
        s_ofm = [Buf(), Buf()]
        sm1 = fsb("l1_sm", [128, 8], F32); s_sm1 = Buf()

        def load_w2(names):
            for wi, wname in enumerate(names):
                for c in range(NC_DC):
                    b.dma_rw("sync", stage32[:], dw[wname][c * 128:(c + 1) * 128, :], writes=[s_stg[0]])
                    b.rw("vector", lambda v, c=c, wi=wi: v.tensor_copy(
                        out=w2[:, c, wi * 1024:(wi + 1) * 1024], in_=stage32[:]),
                        reads=[s_stg[0]], writes=[s_w2])

        emit_ln_stats([fk_tm[0][:], fk_tm[1][:]], s_fk, [axh[0][:], axh[1][:]], s_axh)
        load_w2(["l1_wkt", "l1_wvt"])
        b.rw("gpsimd", lambda g: g.memset(kvE[:], 0.0), writes=[s_kvE])
        b.rw("gpsimd", lambda g: g.memset(kvO[:], 0.0), writes=[s_kvO])
        for p in range(2):
            b.rw("gpsimd", lambda g, p=p: g.memset(v_tm[p][:, :, 64:65], 1.0), writes=[s_vones[p]])
            b.rw("gpsimd", lambda g, p=p: g.memset(v_tm[p][:, :, 129:130], 1.0), writes=[s_vones[p]])

        for t in range(NT):
            p = t % 2
            emit_xhat(t, axh[p][:], s_axh[p])
            for c in range(NC_DC):
                pc = c % 2
                b.rw("tensor", lambda e, c=c, p=p, pc=pc: e.transpose(
                    out=pt[pc][:], in_=axh[p][:, c * 128:(c + 1) * 128], identity=ident[:]),
                    reads=[s_axh[p], s_ident], writes=[s_pt[pc]])
                b.rw("scalar", lambda s, c=c, p=p, pc=pc: s.copy(
                    out=xf_t[p][:, c], in_=pt[pc][:]),
                    reads=[s_pt[pc]], writes=[s_xft[p]])
            for half in range(2):
                for c in range(NC_DC):
                    b.rw("tensor", lambda e, c=c, p=p, half=half: e.matmul(
                        out=pa[half][:], lhsT=xf_t[p][:, c],
                        rhs=w2[:, c, half * 512:(half + 1) * 512], start=(c == 0), stop=False),
                        reads=[s_xft[p], s_w2], writes=[s_pa[half]])
                b.rw("tensor", lambda e, half=half: e.matmul(
                    out=pa[half][:], lhsT=ones_col[:],
                    rhs=w_small["l1_bk"][0:1, half * 512:(half + 1) * 512], start=False, stop=True),
                    reads=[s_ones, s_small["l1_bk"]], writes=[s_pa[half]])
                b.rw("vector", lambda v, half=half, p=p: v.tensor_single_scalar(
                    out=scr32[half][:], in_=pa[half][:], scalar=0.0, op=ALU.min),
                    reads=[s_pa[half]], writes=[s_scr[half]])
                b.rw("scalar", lambda s, half=half: s.activation(
                    out=scr32[half][:], in_=scr32[half][:], func=AF.Exp),
                    reads=[s_scr[half]], writes=[s_scr[half]])
                b.rw("vector", lambda v, half=half, p=p: v.scalar_tensor_tensor(
                    out=fk_tm[p][:, half * 512:(half + 1) * 512], in0=pa[half][:], scalar=0.0,
                    in1=scr32[half][:], op0=ALU.max, op1=ALU.add),
                    reads=[s_pa[half], s_scr[half]], writes=[s_fk[p]])
                for c in range(NC_DC):
                    b.rw("tensor", lambda e, c=c, p=p, half=half: e.matmul(
                        out=pa[2 + half][:], lhsT=xf_t[p][:, c],
                        rhs=w2[:, c, 1024 + half * 512:1024 + (half + 1) * 512], start=(c == 0), stop=False),
                        reads=[s_xft[p], s_w2], writes=[s_pa[2 + half]])
                b.rw("tensor", lambda e, half=half: e.matmul(
                    out=pa[2 + half][:], lhsT=ones_col[:],
                    rhs=w_small["l1_bv"][0:1, half * 512:(half + 1) * 512], start=False, stop=True),
                    reads=[s_ones, s_small["l1_bv"]], writes=[s_pa[2 + half]])
                vsrc = pa[2 + half][:].rearrange("q (a two m) -> q a two m", two=2, m=64)
                b.rw("scalar", lambda s, half=half, p=p, vsrc=vsrc: s.copy(
                    out=v_tm[p][:, 4 * half:4 * half + 4, 0:64], in_=vsrc[:, :, 0, :]),
                    reads=[s_pa[2 + half]], writes=[s_v[p]])
                b.rw("scalar", lambda s, half=half, p=p, vsrc=vsrc: s.copy(
                    out=v_tm[p][:, 4 * half:4 * half + 4, 65:129], in_=vsrc[:, :, 1, :]),
                    reads=[s_pa[2 + half]], writes=[s_v[p]])
            for j in range(8):
                pj = j % 2
                dstp = pnk if pj == 0 else pmu
                s_dstp = s_pnk if pj == 0 else s_pmu
                b.rw("tensor", lambda e, j=j, p=p, dstp=dstp: e.matmul(
                    out=dstp[:], lhsT=fk_tm[p][:, j * 128:(j + 1) * 128],
                    rhs=v_tm[p][:, j, :], start=True, stop=True),
                    reads=[s_fk[p], s_v[p], s_vones[p]], writes=[s_dstp])
                b.rw("vector", lambda v, j=j, dstp=dstp: v.tensor_tensor(
                    out=kvE[0:64, j, :], in0=dstp[0:64, 0:65], in1=kvE[0:64, j, :], op=ALU.add),
                    reads=[s_dstp, s_kvE], writes=[s_kvE])
                b.rw("vector", lambda v, j=j, dstp=dstp: v.tensor_tensor(
                    out=kvO[64:128, j, :], in0=dstp[64:128, 65:130], in1=kvO[64:128, j, :], op=ALU.add),
                    reads=[s_dstp, s_kvO], writes=[s_kvO])

        load_w2(["l1_wqt", "l1_wot"])
        b.rw("gpsimd", lambda g: g.memset(kv2b[:], 0.0), writes=[s_kv2b])
        b.rw("vector", lambda v: v.tensor_copy(out=kv2b[0:64, :, 0:65], in_=kvE[0:64, :, :]),
             reads=[s_kvE], writes=[s_kv2b])
        b.rw("vector", lambda v: v.tensor_copy(out=kv2b[64:128, :, 65:130], in_=kvO[64:128, :, :]),
             reads=[s_kvO], writes=[s_kv2b])
        for g in range(8):
            for i in range(4):
                t = 4 * g + i
                p = i % 2
                emit_xhat(t, axh[p][:], s_axh[p])
                for c in range(NC_DC):
                    pc = c % 2
                    b.rw("tensor", lambda e, c=c, p=p, pc=pc: e.transpose(
                        out=pt[pc][:], in_=axh[p][:, c * 128:(c + 1) * 128], identity=ident[:]),
                        reads=[s_axh[p], s_ident], writes=[s_pt[pc]])
                    b.rw("scalar", lambda s, c=c, i=i, pc=pc: s.copy(
                        out=xf_g[:, c, i * 128:(i + 1) * 128], in_=pt[pc][:]),
                        reads=[s_pt[pc]], writes=[s_xfg])
            for j in range(NC_DC):
                pj = j % 2
                for c in range(NC_DC):
                    b.rw("tensor", lambda e, j=j, c=c, pj=pj: e.matmul(
                        out=pa[pj][:], lhsT=w2[:, c, j * 128:(j + 1) * 128],
                        rhs=xf_g[:, c, :], start=(c == 0), stop=False),
                        reads=[s_w2, s_xfg], writes=[s_pa[pj]])
                b.rw("tensor", lambda e, j=j, pj=pj: e.matmul(
                    out=pa[pj][:], lhsT=w_small["l1_bq"][0:1, j * 128:(j + 1) * 128],
                    rhs=ones_row[:], start=False, stop=True),
                    reads=[s_small["l1_bq"], s_onesr], writes=[s_pa[pj]])
                b.rw("vector", lambda v, pj=pj: v.tensor_single_scalar(
                    out=scr32[pj][:], in_=pa[pj][:], scalar=0.0, op=ALU.min),
                    reads=[s_pa[pj]], writes=[s_scr[pj]])
                b.rw("scalar", lambda s, pj=pj: s.activation(
                    out=scr32[pj][:], in_=scr32[pj][:], func=AF.Exp),
                    reads=[s_scr[pj]], writes=[s_scr[pj]])
                b.rw("vector", lambda v, j=j, pj=pj: v.scalar_tensor_tensor(
                    out=fq_g[:, j, :], in0=pa[pj][:], scalar=0.0,
                    in1=scr32[pj][:], op0=ALU.max, op1=ALU.add),
                    reads=[s_pa[pj], s_scr[pj]], writes=[s_fqg])
            for tt in range(4):
                for j in range(NC_DC):
                    pj = (4 * tt + j) % 2 + 2
                    b.rw("tensor", lambda e, j=j, tt=tt, pj=pj: e.matmul(
                        out=pa[pj][:, 0:130], lhsT=fq_g[:, j, tt * 128:(tt + 1) * 128],
                        rhs=kv2b[:, j, :], start=True, stop=True),
                        reads=[s_fqg, s_kv2b], writes=[s_pa[pj]])
                    for parity in range(2):
                        h = 2 * j + parity
                        dcol = 64 if parity == 0 else 129
                        vcol = 0 if parity == 0 else 65
                        b.rw("vector", lambda v, pj=pj, dcol=dcol, parity=parity: v.tensor_scalar(
                            out=sm1[:, parity:parity + 1], in0=pa[pj][:, dcol:dcol + 1],
                            scalar1=EPS_ATT, scalar2=0.0, op0=ALU.max, op1=ALU.add),
                            reads=[s_pa[pj]], writes=[s_sm1])
                        b.rw("vector", lambda v, parity=parity: v.reciprocal(
                            out=sm1[:, 2 + parity:3 + parity], in_=sm1[:, parity:parity + 1]),
                            reads=[s_sm1], writes=[s_sm1])
                        b.rw("scalar", lambda s, tt=tt, h=h, pj=pj, vcol=vcol, parity=parity: s.activation(
                            out=out_tm[:, tt, h * 64:(h + 1) * 64], in_=pa[pj][:, vcol:vcol + 64],
                            func=AF.Identity, scale=sm1[:, 2 + parity:3 + parity]),
                            reads=[s_pa[pj], s_sm1], writes=[s_out])
            for tt in range(4):
                t = 4 * g + tt
                po = tt % 2
                for c in range(NC_DC):
                    pc = c % 2
                    b.rw("tensor", lambda e, c=c, tt=tt, pc=pc: e.transpose(
                        out=pt[pc][:], in_=out_tm[:, tt, c * 128:(c + 1) * 128], identity=ident[:]),
                        reads=[s_out, s_ident], writes=[s_pt[pc]])
                    b.rw("scalar", lambda s, c=c, po=po, pc=pc: s.copy(
                        out=ofm[po][:, c], in_=pt[pc][:]),
                        reads=[s_pt[pc]], writes=[s_ofm[po]])
                for o in range(2):
                    b.rw("tensor", lambda e, o=o, po=po, c=0: e.matmul(
                        out=pa[o][:], lhsT=ofm[po][:, 0],
                        rhs=w2[:, 0, 1024 + o * 512:1024 + (o + 1) * 512], start=True, stop=False),
                        reads=[s_ofm[po], s_w2], writes=[s_pa[o]])
                    for c in range(1, NC_DC):
                        b.rw("tensor", lambda e, c=c, o=o, po=po: e.matmul(
                            out=pa[o][:], lhsT=ofm[po][:, c],
                            rhs=w2[:, c, 1024 + o * 512:1024 + (o + 1) * 512], start=False, stop=False),
                            reads=[s_ofm[po], s_w2], writes=[s_pa[o]])
                    b.rw("tensor", lambda e, o=o: e.matmul(
                        out=pa[o][:], lhsT=ones_col[:],
                        rhs=w_small["l1_bo"][0:1, o * 512:(o + 1) * 512], start=False, stop=True),
                        reads=[s_ones, s_small["l1_bo"]], writes=[s_pa[o]])
                    b.rw("vector", lambda v, t=t, o=o: v.tensor_tensor(
                        out=stream[:, t, o * 512:(o + 1) * 512], in0=pa[o][:],
                        in1=stream[:, t, o * 512:(o + 1) * 512], op=ALU.add),
                        reads=[s_pa[o], s_stream[t]], writes=[s_stream[t]])
        b.barrier()
        fs.close()

    def emit_hamprobe():
        fs = ExitStack()
        hw8 = fs.enter_context(nc.sbuf_tensor("hp_w8", [128, 8, 128], BF16, side="right"))
        ha = fs.enter_context(nc.sbuf_tensor("hp_a", [128, 512], BF16, side="right"))
        s_hw = Buf(); s_ha = Buf()
        b.rw("gpsimd", lambda g: g.memset(hw8[:], 0.5), writes=[s_hw])
        b.rw("gpsimd", lambda g: g.memset(ha[:], 0.5), writes=[s_ha])
        for g in range(125):
            pp = g % 2
            for c in range(NC_DC):
                b.rw("tensor", lambda e, c=c, pp=pp: e.matmul(
                    out=pa[pp][:], lhsT=hw8[:, c], rhs=ha[:], start=(c == 0), stop=(c == 7)),
                    reads=[s_hw, s_ha], writes=[s_pa[pp]])
        b.barrier()
        fs.close()

    if "hamprobe" in phases:
        emit_hamprobe()
    if "ffn0a" in phases:
        emit_ffn0()
    if "l0attn" in phases:
        emit_l0attn()
    if "ffn0b" in phases:
        emit_ffn0()
    if "wq" in phases:
        emit_wquant()
    if "ffn1a" in phases:
        emit_ffn1()
    if "l1attn" in phases:
        emit_l1attn()
    if "ffn1b" in phases:
        emit_ffn1()

    emit_output()
    return b


def make_shared(inputs):
    f32c = lambda a: np.ascontiguousarray(np.asarray(a, np.float32))
    return {
        "l0_wq": f32c(inputs["L0_attn_Wq"]), "l0_wk": f32c(inputs["L0_attn_Wk"]),
        "l0_wv": f32c(inputs["L0_attn_Wv"]),
        "l0_bq": f32c(inputs["L0_attn_bq"]).reshape(1, -1),
        "l0_bk": f32c(inputs["L0_attn_bk"]).reshape(1, -1),
        "l0_bv": f32c(inputs["L0_attn_bv"]).reshape(1, -1),
        "l0_beta": f32c(inputs["L0_attn_beta"]).reshape(1, -1),
        "f0_win": f32c(inputs["L0_ffn_Win"]),
        "f0_bin": f32c(inputs["L0_ffn_bin_"]).reshape(1, -1),
        "f0_w1": f32c(inputs["L0_ffn_W1"]).reshape(1, -1),
        "f0_b1": f32c(inputs["L0_ffn_b1"]).reshape(1, -1),
        "f0_w2": f32c(inputs["L0_ffn_W2"]).reshape(1, -1),
        "f0_b2": f32c(inputs["L0_ffn_b2"]).reshape(1, -1),
        "f0_wout": f32c(inputs["L0_ffn_Wout"]),
        "f0_bout": f32c(inputs["L0_ffn_bout"]).reshape(1, -1),
        "l1_wqt": f32c(inputs["L1_attn_Wq"].T), "l1_wkt": f32c(inputs["L1_attn_Wk"].T),
        "l1_wvt": f32c(inputs["L1_attn_Wv"].T), "l1_wot": f32c(inputs["L1_attn_Wo"].T),
        "l1_bq": f32c(inputs["L1_attn_bq"]).reshape(1, -1),
        "l1_bk": f32c(inputs["L1_attn_bk"]).reshape(1, -1),
        "l1_bv": f32c(inputs["L1_attn_bv"]).reshape(1, -1),
        "l1_bo": f32c(inputs["L1_attn_bo"]).reshape(1, -1),
        "f1_wg": f32c(inputs["L1_ffn_w_gate"]), "f1_wu": f32c(inputs["L1_ffn_w_up"]),
        "f1_wd": f32c(inputs["L1_ffn_w_down"]),
    }


def kernel(**inputs):
    xin = np.ascontiguousarray(inputs["L0_attn_x"].astype(np.float32))
    shared = make_shared(inputs)
    import os
    phases = tuple(os.environ.get("KPHASES", "ffn0a,l0attn,ffn0b,wq,ffn1a,l1attn,ffn1b").split(","))
    nc = bass.Bass("TRN2", target_bir_lowering=False)
    b = build(nc, phases)
    with ExitStack() as st:
        block = st.enter_context(nc.Block())
        b.finalize(block)
    b.close()
    in_maps = [dict(shared, x=np.ascontiguousarray(xin[i])) for i in range(8)]
    res = run_bass_kernel_spmd(nc, in_maps, core_ids=list(range(8)))
    out = np.stack([np.asarray(r["out"], np.float32) for r in res.results], axis=0)
    return out


# version 11 (final)
# speedup vs baseline: 1.2028x; 1.2028x over previous
import numpy as np
import ml_dtypes
import concourse.bass as bass
import concourse.mybir as mybir
from concourse.bass_utils import run_bass_kernel_spmd
from contextlib import ExitStack

F32 = mybir.dt.float32
BF16 = mybir.dt.bfloat16
FP16 = mybir.dt.float16
U8 = mybir.dt.uint8
I16 = mybir.dt.int16
ALU = mybir.AluOpType
AF = mybir.ActivationFunctionType
AX = mybir.AxisListType

T = 4096
D = 1024
NT = T // 128
NC_DC = D // 128
I_FFN = 2816
M05, M1, M2 = 6291456.0, 12582912.0, 25165824.0
EPS_LN = 1e-5
EPS_ATT = 1e-6

ENGS = ("tensor", "vector", "scalar", "gpsimd", "sync")
HAS_GP_REDUCE = False


class B:

    def __init__(self, nc):
        self.nc = nc
        self.ops = {e: [] for e in ENGS}
        self.stack = ExitStack()
        self.sem = {}
        self.cnt = {}
        self.hwm = {e: {} for e in ENGS}
        for e in ENGS:
            self.sem[e] = self.stack.enter_context(nc.semaphore("sem_" + e))
            self.cnt[e] = 0
        self.NDMA = 8
        self.dmac = {}
        for q in ("dsync", "dgp", "dscal"):
            for i in range(self.NDMA):
                s = f"{q}{i}"
                self.sem[s] = self.stack.enter_context(nc.semaphore("sem_" + s))
                self.cnt[s] = 0
            self.dmac[q] = 0

    def close(self):
        self.stack.close()

    def _wait(self, eng, tok, force=False):
        if tok is None:
            return
        sname, val = tok
        if val <= 0:
            return
        if not force and sname == eng == "tensor":
            return
        if self.hwm[eng].get(sname, 0) >= val:
            return
        self.hwm[eng][sname] = val
        sem = self.sem[sname]
        self.ops[eng].append(lambda e, s=sem, v=val: e.wait_ge(s, v))

    def emit(self, eng, fn, deps=(), inc=False):
        for d in deps:
            self._wait(eng, d)
        if inc:
            self.cnt[eng] += 1
            val = self.cnt[eng]
            sem = self.sem[eng]

            def wrapped(e, fn=fn, sem=sem):
                r = fn(e)
                assert r is not None, "inc=True needs fn to return the last instruction"
                r.then_inc(sem, 1)
            self.ops[eng].append(wrapped)
            return (eng, val)
        self.ops[eng].append(fn)
        return None

    def dma(self, qeng, out, in_, deps=()):
        q = {"sync": "dsync", "gpsimd": "dgp", "scalar": "dscal"}[qeng]
        for d in deps:
            self._wait(qeng, d)
        sname = f"{q}{self.dmac[q] % self.NDMA}"
        self.dmac[q] += 1
        self.cnt[sname] += 16
        val = self.cnt[sname]
        sem = self.sem[sname]
        self.ops[qeng].append(lambda e, o=out, i=in_, s=sem: e.dma_start(out=o, in_=i).then_inc(s, 16))
        return (sname, val)

    def tok(self, eng):
        return (eng, self.cnt[eng])

    def rw(self, eng, fn, reads=(), writes=(), extra=(), inc=True):
        assert inc, "rw requires inc=True so Buf tokens are real"

        """Emit with automatic buffer dependency tracking.
        reads/writes are Buf objects; conflicting accesses get sem edges
        (required by the grading simulator's race model even same-engine)."""
        deps = list(extra)
        for bf in reads:
            if bf.w is not None:
                deps.append(bf.w)
        for bf in writes:
            if bf.w is not None:
                deps.append(bf.w)
            deps.extend(bf.readers())
        t = self.emit(eng, fn, deps=deps, inc=inc)
        for bf in reads:
            bf.add_reader(t if t is not None else self.tok(eng))
        for bf in writes:
            bf.w = t if t is not None else self.tok(eng)
            bf.r = {}
        return t

    def dma_rw(self, qeng, out, in_, reads=(), writes=(), extra=()):
        deps = list(extra)
        for bf in reads:
            if bf.w is not None:
                deps.append(bf.w)
        for bf in writes:
            if bf.w is not None:
                deps.append(bf.w)
            deps.extend(bf.readers())
        t = self.dma(qeng, out, in_, deps=deps)
        for bf in reads:
            bf.add_reader(t)
        for bf in writes:
            bf.w = t
            bf.r = {}
        return t

    def barrier(self):
        dtoks = [(f"{q}{i}", self.cnt[f"{q}{i}"]) for q in ("dsync", "dgp", "dscal")
                 for i in range(self.NDMA)]
        for e in ENGS:
            self._wait(e, (e, self.cnt[e]), force=True)
            self.emit(e, lambda eng: eng.nop(), inc=True)
        toks2 = [(e, self.cnt[e]) for e in ENGS]
        for e in ENGS:
            for t in toks2:
                if t[0] != e:
                    self._wait(e, t)
            for t in dtoks:
                self._wait(e, t)

    def finalize(self, block):
        for ename, lst in self.ops.items():
            dec = getattr(block, ename)

            def make(lst):
                def body(eng):
                    for f in lst:
                        f(eng)
                return body
            dec(make(lst))


class Buf:
    __slots__ = ("w", "r")

    def __init__(self):
        self.w = None
        self.r = {}

    def add_reader(self, tok):
        if tok is None:
            return
        s, v = tok
        if self.r.get(s, 0) < v:
            self.r[s] = v

    def readers(self):
        return [(s, v) for s, v in self.r.items()]


def ceil_div(a, b):
    return (a + b - 1) // b


def build(nc, phases):
    dx = nc.dram_tensor("x", [T, D], F32, kind="ExternalInput").ap()
    dout = nc.dram_tensor("out", [T, D], F32, kind="ExternalOutput").ap()
    dw = {}
    for name, shape in [
        ("l0_wq", [D, D]), ("l0_wk", [D, D]), ("l0_wv", [D, D]),
        ("l0_bq", [1, D]), ("l0_bk", [1, D]), ("l0_bv", [1, D]), ("l0_beta", [1, 8]),
        ("f0_win", [D, 64]), ("f0_bin", [1, 64]), ("f0_w1", [1, 1024]), ("f0_b1", [1, 1024]),
        ("f0_w2", [1, 1024]), ("f0_b2", [1, 64]), ("f0_wout", [64, D]), ("f0_bout", [1, D]),
        ("l1_wqt", [D, D]), ("l1_wkt", [D, D]), ("l1_wvt", [D, D]), ("l1_wot", [D, D]),
        ("l1_bq", [1, D]), ("l1_bk", [1, D]), ("l1_bv", [1, D]), ("l1_bo", [1, D]),
        ("f1_wg", [I_FFN, D]), ("f1_wu", [I_FFN, D]), ("f1_wd", [D, I_FFN]),
    ]:
        dw[name] = nc.dram_tensor(name, shape, F32, kind="ExternalInput").ap()
    d_wgq = nc.dram_tensor("wgq", [NC_DC * I_FFN * 128], BF16, kind="Internal").ap()
    d_wuq = nc.dram_tensor("wuq", [NC_DC * I_FFN * 128], BF16, kind="Internal").ap()
    d_wdq = nc.dram_tensor("wdq", [22, 128, D], BF16, kind="Internal").ap()

    b = B(nc)
    sb = lambda name, shape, dt=F32: b.stack.enter_context(nc.sbuf_tensor(name, shape, dt))
    ps = lambda name, shape, dt=F32: b.stack.enter_context(nc.psum_tensor(name, shape, dt))

    stream = sb("stream", [128, NT, D], BF16)
    s_stream = [Buf() for _ in range(NT)]
    ident = sb("ident", [128, 128], BF16); s_ident = Buf()
    lmask = sb("lmask", [128, 128], BF16); s_lmask = Buf()
    g8 = sb("g8", [128, 8]); s_g8 = Buf()
    omg8 = sb("omg8", [128, 8]); s_omg8 = Buf()
    s_iot = Buf()
    ones_col = sb("ones_col", [1, 128], BF16); s_ones = Buf()
    st_s1 = sb("st_s1", [128, NT]); s_s1c = [Buf() for _ in range(NT)]
    st_s2 = sb("st_s2", [128, NT]); s_s2c = [Buf() for _ in range(NT)]
    st_t1 = sb("st_t1", [128, NT]); s_t1 = Buf()
    st_rstd = sb("st_rstd", [128, NT]); s_rstd = Buf()
    st_bias = sb("st_bias", [128, NT]); s_bias = Buf()
    s_s1 = Buf(); s_s2 = Buf()

    INV_SQRT_DK = 1.0 / float(np.sqrt(np.float32(128.0)))
    for i, cv in enumerate([EPS_LN, EPS_ATT, INV_SQRT_DK]):
        cstt = sb(f"cstf{i}", [128, 1], F32)
        b.emit("gpsimd", lambda g, cstt=cstt, cv=cv: g.memset(cstt[:], cv), inc=True)
        nc.const_aps.register(F32, cv, cstt[:])

    w_small, s_small = {}, {}
    for name, shape in [
        ("f0_win", [128, NC_DC, 64]), ("f0_wout", [64, D]),
        ("f0_w1", [128, 1024]), ("f0_b1", [128, 1024]), ("f0_w2", [128, 1024]),
        ("f0_b2", [128, 64]),
        ("f0_bin", [1, 64]), ("f0_bout", [1, D]),
        ("l0_bq", [1, D]), ("l0_bk", [1, D]), ("l0_bv", [1, D]),
        ("l1_bq", [1, D]), ("l1_bk", [1, D]), ("l1_bv", [1, D]), ("l1_bo", [1, D]),
    ]:
        w_small[name] = sb("sw_" + name, shape, BF16)
        s_small[name] = Buf()

    stage32 = sb("stage32", [128, D])
    s_stg = [Buf()]

    import os as _os
    _skipx = _os.environ.get("SKIPX", "0") == "1"
    _skipw = _os.environ.get("SKIPW", "0") == "1"
    for t in (range(0) if _skipx else range(NT)):
        b.dma_rw("sync", stage32[:], dx[t * 128:(t + 1) * 128, :], writes=[s_stg[0]])
        b.rw("gpsimd", lambda g, t=t: g.tensor_copy(out=stream[:, t], in_=stage32[:]),
             reads=[s_stg[0]], writes=[s_stream[t]])

    b.rw("gpsimd", lambda g: g.memset(ones_col[:], 1.0), writes=[s_ones])

    def load_small(name, dramname, rows, cols):
        if _skipw:
            return
        b.dma_rw("sync", stage32[:rows, 0:cols], dw[dramname][0:rows, 0:cols], writes=[s_stg[0]])
        b.rw("vector", lambda v: v.tensor_copy(out=w_small[name][0:rows, 0:cols], in_=stage32[:rows, 0:cols]),
             reads=[s_stg[0]], writes=[s_small[name]])

    for nm, dn, r, c in [("f0_wout", "f0_wout", 64, D), ("f0_bin", "f0_bin", 1, 64),
                         ("f0_bout", "f0_bout", 1, D),
                         ("l0_bq", "l0_bq", 1, D), ("l0_bk", "l0_bk", 1, D), ("l0_bv", "l0_bv", 1, D),
                         ("l1_bq", "l1_bq", 1, D), ("l1_bk", "l1_bk", 1, D), ("l1_bv", "l1_bv", 1, D),
                         ("l1_bo", "l1_bo", 1, D)]:
        load_small(nm, dn, r, c)
    if not _skipw:
      b.dma_rw("sync", stage32[:, 0:512].rearrange("p (c m) -> p c m", m=64),
             dw["f0_win"].rearrange("(c p) m -> p c m", p=128), writes=[s_stg[0]])
    b.rw("vector", lambda v: v.tensor_copy(out=w_small["f0_win"][:], in_=stage32[:, 0:512].rearrange("p (c m) -> p c m", m=64)),
         reads=[s_stg[0]], writes=[s_small["f0_win"]])

    setup_stack = ExitStack()
    ssb = lambda name, shape, dt=F32: setup_stack.enter_context(nc.sbuf_tensor(name, shape, dt, side="right"))
    iot = ssb("iot", [128, 128])
    b.rw("gpsimd", lambda g: g.iota(iot[:], [[1, 128]], base=0, channel_multiplier=-1,
                                    allow_small_or_imprecise_dtypes=True), writes=[s_iot])
    b.rw("vector", lambda v: v.tensor_single_scalar(out=ident[:], in_=iot[:], scalar=0.0, op=ALU.is_equal),
         reads=[s_iot], writes=[s_ident])
    b.rw("vector", lambda v: v.tensor_single_scalar(out=lmask[:], in_=iot[:], scalar=0.0, op=ALU.is_le),
         reads=[s_iot], writes=[s_lmask])
    sps = lambda name, shape, dt=F32: setup_stack.enter_context(nc.psum_tensor(name, shape, dt, side="right"))
    rows32 = {n: ssb("r32_" + n, [1, w]) for n, w in
              [("w1", 1024), ("b1", 1024), ("w2", 1024), ("b2", 64), ("beta", 8)]}
    rows_b = {n: ssb("rb_" + n, [1, w], BF16) for n, w in
              [("w1", 1024), ("b1", 1024), ("w2", 1024), ("b2", 64)]}
    s_rows32 = {n: Buf() for n in rows32}
    s_rows_b = {n: Buf() for n in rows_b}
    for n, dn in [("w1", "f0_w1"), ("b1", "f0_b1"), ("w2", "f0_w2"), ("b2", "f0_b2"), ("beta", "l0_beta")]:
        b.dma_rw("sync", rows32[n][:], dw[dn], writes=[s_rows32[n]])
    for n in ["w1", "b1", "w2", "b2"]:
        b.rw("vector", lambda v, n=n: v.tensor_copy(out=rows_b[n][:], in_=rows32[n][:]),
             reads=[s_rows32[n]], writes=[s_rows_b[n]])
    g_row = ssb("g_row", [1, 8]); s_growf = Buf()
    g_rowb = ssb("g_rowb", [1, 8], BF16); s_growb = Buf()
    b.rw("scalar", lambda s: s.activation(out=g_row[:], in_=rows32["beta"][:], func=AF.Sigmoid),
         reads=[s_rows32["beta"]], writes=[s_growf])
    b.rw("vector", lambda v: v.tensor_copy(out=g_rowb[:], in_=g_row[:]), reads=[s_growf], writes=[s_growb])
    pbc = sps("pbc", [128, 512], F32); s_pbc = Buf()

    def bcast_row(src_row, s_src, width, dst, s_dst):
        for c0 in range(0, width, 512):
            w = min(512, width - c0)
            b.rw("tensor", lambda t, c0=c0, w=w: t.matmul(out=pbc[:, 0:w], lhsT=ones_col[:],
                 rhs=src_row[:, c0:c0 + w], start=True, stop=True),
                 reads=[s_src, s_ones], writes=[s_pbc])
            b.rw("vector", lambda v, c0=c0, w=w: v.tensor_copy(out=dst[:, c0:c0 + w], in_=pbc[:, 0:w]),
                 reads=[s_pbc], writes=[s_dst])

    bcast_row(rows_b["w1"][:], s_rows_b["w1"], 1024, w_small["f0_w1"][:], s_small["f0_w1"])
    bcast_row(rows_b["b1"][:], s_rows_b["b1"], 1024, w_small["f0_b1"][:], s_small["f0_b1"])
    bcast_row(rows_b["w2"][:], s_rows_b["w2"], 1024, w_small["f0_w2"][:], s_small["f0_w2"])
    bcast_row(rows_b["b2"][:], s_rows_b["b2"], 64, w_small["f0_b2"][:], s_small["f0_b2"])
    bcast_row(g_rowb[:], s_growb, 8, g8[:], s_g8)
    b.rw("vector", lambda v: v.tensor_scalar(out=omg8[:], in0=g8[:], scalar1=-1.0, scalar2=1.0,
         op0=ALU.mult, op1=ALU.add), reads=[s_g8], writes=[s_omg8])

    b.barrier()
    setup_stack.close()

    def emit_output():
        last = None
        for t in range(NT):
            b.rw("gpsimd", lambda g, t=t: g.tensor_copy(out=stage32[:], in_=stream[:, t]),
                 reads=[s_stream[t]], writes=[s_stg[0]])
            last = b.dma_rw("sync", dout[t * 128:(t + 1) * 128, :], stage32[:],
                            reads=[s_stg[0]])
        for q in ("dsync", "dgp", "dscal"):
            for i in range(b.NDMA):
                sname = f"{q}{i}"
                if b.cnt[sname] > 0:
                    b._wait("sync", (sname, b.cnt[sname]))

    def emit_hamprobe2():
        fs = ExitStack()
        hw8 = fs.enter_context(nc.sbuf_tensor("hp2_w8", [128, 8, 128], BF16, side="right"))
        ha = fs.enter_context(nc.sbuf_tensor("hp2_a", [128, 512], BF16, side="right"))
        pA = fs.enter_context(nc.psum_tensor("hp2_pa", [128, 512], F32, side="right"))
        pB = fs.enter_context(nc.psum_tensor("hp2_pb", [128, 512], F32, side="right"))
        s_hw = Buf(); s_ha = Buf(); s_qa = Buf(); s_qb = Buf()
        b.rw("gpsimd", lambda g: g.memset(hw8[:], 0.5), writes=[s_hw])
        b.rw("gpsimd", lambda g: g.memset(ha[:], 0.5), writes=[s_ha])
        for g in range(125):
            pp, s_pp = (pA, s_qa) if g % 2 == 0 else (pB, s_qb)
            for c in range(NC_DC):
                b.rw("tensor", lambda e, c=c, pp=pp: e.matmul(
                    out=pp[:], lhsT=hw8[:, c], rhs=ha[:], start=(c == 0), stop=(c == 7)),
                    reads=[s_hw, s_ha], writes=[s_pp])
        b.barrier()
        fs.close()

    if "min2" in phases:
        emit_hamprobe2()
        emit_output()
        return b
    if "identity" in phases or "min" in phases:
        emit_output()
        return b

    pa = [ps(f"pa{i}", [128, 512], F32) for i in range(4)]
    s_pa = [Buf() for _ in range(4)]
    pnk = ps("pnk", [128, 130], F32); s_pnk = Buf()
    pmu = ps("pmu", [128, 130], F32); s_pmu = Buf()
    pt = [ps(f"pt{i}", [128, 128], BF16) for i in range(2)]
    s_pt = [Buf(), Buf()]
    ones_row = sb("ones_row", [1, 512], BF16); s_onesr = Buf()
    b.rw("gpsimd", lambda g: g.memset(ones_row[:], 1.0), writes=[s_onesr])

    def emit_ln_stats(scrA, s_scrA, scrB, s_scrB):
        for t in range(NT):
            p = t % 2
            b.rw("vector", lambda v, t=t, p=p: v.tensor_scalar(
                out=scrA[p], in0=stream[:, t], scalar1=1.0, scalar2=0.0,
                op0=ALU.mult, op1=ALU.add, accum_out=st_s1[:, t:t + 1]),
                reads=[s_stream[t]], writes=[s_scrA[p], s_s1c[t]])
            b.rw("vector", lambda v, t=t, p=p: v.scalar_tensor_tensor(
                out=scrB[p], in0=stream[:, t], scalar=1.0, in1=stream[:, t],
                op0=ALU.mult, op1=ALU.mult, accum_out=st_s2[:, t:t + 1]),
                reads=[s_stream[t]], writes=[s_scrB[p], s_s2c[t]])
        b.rw("vector", lambda v: v.tensor_scalar(
            out=st_t1[:], in0=st_s1[:], scalar1=1.0 / D, scalar2=0.0, op0=ALU.mult, op1=ALU.add),
            reads=s_s1c, writes=[s_t1])
        b.rw("vector", lambda v: v.tensor_tensor(out=st_bias[:], in0=st_t1[:], in1=st_t1[:], op=ALU.mult),
            reads=[s_t1], writes=[s_bias])
        b.rw("vector", lambda v: v.scalar_tensor_tensor(
            out=st_s2[:], in0=st_s2[:], scalar=1.0 / D, in1=st_bias[:], op0=ALU.mult, op1=ALU.subtract),
            reads=s_s2c + [s_bias], writes=[s_s2] + s_s2c)
        b.rw("scalar", lambda s: s.activation(out=st_s1[:], in_=st_s2[:], func=AF.Sqrt, bias=EPS_LN, scale=1.0),
            reads=[s_s2], writes=[s_s1] + s_s1c)
        b.rw("vector", lambda v: v.reciprocal(out=st_rstd[:], in_=st_s1[:]),
            reads=[s_s1], writes=[s_rstd])
        b.rw("vector", lambda v: v.scalar_tensor_tensor(
            out=st_bias[:], in0=st_t1[:], scalar=-1.0, in1=st_rstd[:], op0=ALU.mult, op1=ALU.mult),
            reads=[s_t1, s_rstd], writes=[s_bias])

    def emit_xhat(t, dst, s_dst):
        b.rw("scalar", lambda s, t=t: s.activation(
            out=dst, in_=stream[:, t], func=AF.Identity,
            bias=st_bias[:, t:t + 1], scale=st_rstd[:, t:t + 1]),
            reads=[s_stream[t], s_bias, s_rstd], writes=[s_dst])

    _f0_call = [0]

    def emit_ffn0():
        _f0_call[0] += 1
        sfx = f"_c{_f0_call[0]}"
        fs = ExitStack()
        fsb = lambda name, shape, dt=F32: fs.enter_context(nc.sbuf_tensor(name, shape, dt, side="right"))
        f0_xhat = [fsb(f"f0_xhat{i}" + sfx, [128, D], BF16) for i in range(2)]
        f0_xf = [fsb(f"f0_xf{i}" + sfx, [128, NC_DC, 128], BF16) for i in range(2)]
        f0_ub = [fsb(f"f0_ub{i}" + sfx, [128, 1024], BF16) for i in range(2)]
        f0_hb = [fsb(f"f0_hb{i}" + sfx, [128, 1024], BF16) for i in range(2)]
        f0_g32 = [fsb(f"f0_g32{i}" + sfx, [128, 64]) for i in range(2)]
        f0_g2b = [fsb(f"f0_g2b{i}" + sfx, [128, 64], BF16) for i in range(2)]
        f0_gfm = [fsb(f"f0_gfm{i}" + sfx, [64, 128], BF16) for i in range(2)]
        sf0 = {n: [Buf(), Buf()] for n in ("xhat", "xf", "ub", "hb", "g32", "g2b", "gfm")}
        emit_ln_stats([f0_ub[0][:], f0_ub[1][:]], sf0["ub"], [f0_hb[0][:], f0_hb[1][:]], sf0["hb"])
        for t in range(NT):
            p = t % 2
            emit_xhat(t, f0_xhat[p][:], sf0["xhat"][p])
            for c in range(NC_DC):
                pc = c % 2
                b.rw("tensor", lambda e, c=c, p=p, pc=pc: e.transpose(
                    out=pt[pc][:], in_=f0_xhat[p][:, c * 128:(c + 1) * 128], identity=ident[:]),
                    reads=[sf0["xhat"][p], s_ident], writes=[s_pt[pc]])
                b.rw("scalar", lambda s, c=c, p=p, pc=pc: s.copy(
                    out=f0_xf[p][:, c], in_=pt[pc][:]),
                    reads=[s_pt[pc]], writes=[sf0["xf"][p]])
            for c in range(NC_DC):
                b.rw("tensor", lambda e, c=c, p=p: e.matmul(
                    out=pa[p][:, 0:64], lhsT=f0_xf[p][:, c], rhs=w_small["f0_win"][:, c],
                    start=(c == 0), stop=False),
                    reads=[sf0["xf"][p], s_small["f0_win"]], writes=[s_pa[p]])
            b.rw("tensor", lambda e, p=p: e.matmul(
                out=pa[p][:, 0:64], lhsT=ones_col[:], rhs=w_small["f0_bin"][0:1, :],
                start=False, stop=True),
                reads=[s_ones, s_small["f0_bin"]], writes=[s_pa[p]])
            s_bv = pa[p][:, 0:64].rearrange("q (m o) -> q m o", o=1).broadcast_to([128, 64, 16])
            b.rw("vector", lambda v, p=p, s_bv=s_bv: v.tensor_tensor(
                out=f0_ub[p][:].rearrange("q (m k) -> q m k", k=16), in0=s_bv,
                in1=w_small["f0_w1"][:].rearrange("q (m k) -> q m k", k=16), op=ALU.mult),
                reads=[s_pa[p], s_small["f0_w1"]], writes=[sf0["ub"][p]])
            b.rw("vector", lambda v, p=p: v.tensor_tensor(
                out=f0_hb[p][:], in0=f0_ub[p][:], in1=w_small["f0_b1"][:], op=ALU.add),
                reads=[sf0["ub"][p], s_small["f0_b1"]], writes=[sf0["hb"][p]])
            b.rw("vector", lambda v, p=p: v.tensor_single_scalar(
                out=f0_hb[p][:], in_=f0_hb[p][:], scalar=0.0, op=ALU.max),
                reads=[sf0["hb"][p]], writes=[sf0["hb"][p]])
            b.rw("vector", lambda v, p=p: v.tensor_tensor(
                out=f0_ub[p][:], in0=f0_hb[p][:], in1=w_small["f0_w2"][:], op=ALU.mult),
                reads=[sf0["hb"][p], s_small["f0_w2"]], writes=[sf0["ub"][p]])
            b.rw("vector", lambda v, p=p: v.tensor_reduce(
                out=f0_g32[p][:], in_=f0_ub[p][:].rearrange("q (m k) -> q m k", k=16),
                axis=AX.X, op=ALU.add),
                reads=[sf0["ub"][p]], writes=[sf0["g32"][p]])
            b.rw("vector", lambda v, p=p: v.tensor_tensor(
                out=f0_g2b[p][:], in0=f0_g32[p][:], in1=w_small["f0_b2"][:], op=ALU.add),
                reads=[sf0["g32"][p], s_small["f0_b2"]], writes=[sf0["g2b"][p]])
            pc = t % 2
            b.rw("tensor", lambda e, p=p, pc=pc: e.transpose(
                out=pt[pc][0:64, :], in_=f0_g2b[p][:], identity=ident[:]),
                reads=[sf0["g2b"][p], s_ident], writes=[s_pt[pc]])
            b.rw("scalar", lambda s, p=p, pc=pc: s.copy(out=f0_gfm[p][:], in_=pt[pc][0:64, :]),
                reads=[s_pt[pc]], writes=[sf0["gfm"][p]])
            for o in range(2):
                pq = 2 + o
                b.rw("tensor", lambda e, p=p, o=o, pq=pq: e.matmul(
                    out=pa[pq][:], lhsT=f0_gfm[p][:],
                    rhs=w_small["f0_wout"][0:64, o * 512:(o + 1) * 512], start=True, stop=False),
                    reads=[sf0["gfm"][p], s_small["f0_wout"]], writes=[s_pa[pq]])
                b.rw("tensor", lambda e, p=p, o=o, pq=pq: e.matmul(
                    out=pa[pq][:], lhsT=ones_col[:],
                    rhs=w_small["f0_bout"][0:1, o * 512:(o + 1) * 512], start=False, stop=True),
                    reads=[s_ones, s_small["f0_bout"]], writes=[s_pa[pq]])
                b.rw("vector", lambda v, t=t, o=o, pq=pq: v.scalar_tensor_tensor(
                    out=stream[:, t, o * 512:(o + 1) * 512], in0=pa[pq][:],
                    scalar=0.5, in1=stream[:, t, o * 512:(o + 1) * 512],
                    op0=ALU.mult, op1=ALU.add),
                    reads=[s_pa[pq], s_stream[t]], writes=[s_stream[t]])
        b.barrier()
        fs.close()

    def emit_l0attn():
        H, L, NSEG = 8, 512, T // 512
        fs = ExitStack()
        fsb = lambda name, shape, dt=BF16: fs.enter_context(nc.sbuf_tensor(name, shape, dt, side="right"))
        wq_sb = fsb("a_wq", [128, NC_DC, D]); s_wq = Buf()
        wk_sb = fsb("a_wk", [128, NC_DC, D]); s_wk = Buf()
        wv_sb = fsb("a_wv", [128, NC_DC, D]); s_wv = Buf()
        qf = fsb("a_qf", [128, H, L]); s_qf = Buf()
        kf = fsb("a_kf", [128, H, L]); s_kf = Buf()
        sqf = fsb("a_sqf", [128, H, L]); s_sqf = Buf()
        skf = fsb("a_skf", [128, H, L]); s_skf = Buf()
        xfo = fsb("a_xfo", [128, 4096])
        s_xfo = Buf()
        vt = fsb("a_vt", [128, 4, 1024]); s_vt = Buf()
        vd = fsb("a_vd", [128, 4, 129])
        s_vd = [Buf() for _ in range(4)]
        s_vd1 = Buf()
        m32 = fsb("a_m32", [128, H, 129], F32); s_m32 = Buf()
        mbf = fsb("a_mbf", [128, H, 129]); s_mbf = Buf()
        esb = fsb("a_esb", [128, 4, L])
        s_esb = [Buf() for _ in range(4)]
        etb = [fsb("a_etb0", [128, 4, 128])]
        s_etb = [Buf()]
        skt = [fsb("a_skt0", [128, 128])]
        s_skt = [Buf()]
        amg = fsb("a_amg", [128, 4, 128])
        s_amg = [Buf() for _ in range(4)]
        esbf = esb[:].rearrange("p a l -> p (a l)")
        axh = [esbf[:, 0:D], esbf[:, D:2 * D]]

        sm = fsb("a_sm", [128, 4, 16], F32)
        s_sm = [Buf() for _ in range(4)]

        qv = qf[:].rearrange("p h l -> p (h l)")
        kv = kf[:].rearrange("p h l -> p (h l)")
        emit_ln_stats([qv[:, 0:D], qv[:, D:2 * D]], [s_qf, s_qf],
                      [kv[:, 0:D], kv[:, D:2 * D]], [s_kf, s_kf])

        for wname, wsb, s_w in [("l0_wq", wq_sb, s_wq), ("l0_wk", wk_sb, s_wk), ("l0_wv", wv_sb, s_wv)]:
            for c in range(NC_DC):
                b.dma_rw("sync", stage32[:], dw[wname][c * 128:(c + 1) * 128, :], writes=[s_stg[0]])
                b.rw("vector", lambda v, c=c, wsb=wsb: v.tensor_copy(
                    out=wsb[:, c], in_=stage32[:]),
                    reads=[s_stg[0]], writes=[s_w])

        b.rw("gpsimd", lambda g: g.memset(m32[:], 0.0), writes=[s_m32])
        b.rw("vector", lambda v: v.tensor_copy(out=mbf[:], in_=m32[:]), reads=[s_m32], writes=[s_mbf])
        b.rw("gpsimd", lambda g: g.memset(vd[:, :, 128:129], 1.0), writes=[s_vd1])

        INV = 1.0 / float(np.sqrt(np.float32(128.0)))
        for s in range(NSEG):
            xf = xfo[:].rearrange("p (c l) -> p c l", l=L)
            aout = xfo[:].rearrange("p (lt e) -> p lt e", e=1024)
            for i in range(4):
                t = 4 * s + i
                p = i % 2
                b.rw("scalar", lambda sc, t=t, p=p: sc.activation(
                    out=axh[p], in_=stream[:, t], func=AF.Identity,
                    bias=st_bias[:, t:t + 1], scale=st_rstd[:, t:t + 1]),
                    reads=[s_stream[t], s_bias, s_rstd],
                    writes=[s_esb[2 * p], s_esb[2 * p + 1]])
                for c in range(NC_DC):
                    pc = c % 2
                    b.rw("tensor", lambda e, c=c, p=p, pc=pc: e.transpose(
                        out=pt[pc][:], in_=axh[p][:, c * 128:(c + 1) * 128], identity=ident[:]),
                        reads=[s_esb[2 * p], s_esb[2 * p + 1], s_ident], writes=[s_pt[pc]])
                    b.rw("scalar", lambda sc, c=c, i=i, pc=pc: sc.copy(
                        out=xf[:, c, i * 128:(i + 1) * 128], in_=pt[pc][:]),
                        reads=[s_pt[pc]], writes=[s_xfo])
            for dst, s_dst, wsb, s_w, brow in ((qf, s_qf, wq_sb, s_wq, "l0_bq"), (kf, s_kf, wk_sb, s_wk, "l0_bk")):
                for j in range(H):
                    pq = j % 2
                    for c in range(NC_DC):
                        b.rw("tensor", lambda e, j=j, c=c, pq=pq, wsb=wsb: e.matmul(
                            out=pa[pq][:], lhsT=wsb[:, c, j * 128:(j + 1) * 128],
                            rhs=xf[:, c, :], start=(c == 0), stop=False),
                            reads=[s_w, s_xfo], writes=[s_pa[pq]])
                    b.rw("tensor", lambda e, j=j, pq=pq, brow=brow: e.matmul(
                        out=pa[pq][:], lhsT=w_small[brow][0:1, j * 128:(j + 1) * 128],
                        rhs=ones_row[:], start=False, stop=True),
                        reads=[s_small[brow], s_onesr], writes=[s_pa[pq]])
                    b.rw("vector", lambda v, j=j, pq=pq, dst=dst: v.tensor_copy(
                        out=dst[:, j], in_=pa[pq][:]), reads=[s_pa[pq]], writes=[s_dst])
            for src_t, s_src, dst_t, s_dst in ((qf, s_qf, sqf, s_sqf), (kf, s_kf, skf, s_skf)):
                for j in range(H):
                    pe = j % 4
                    b.rw("vector", lambda v, j=j, pe=pe, src_t=src_t: v.tensor_single_scalar(
                        out=esb[:, pe, :], in_=src_t[:, j], scalar=0.0, op=ALU.min),
                        reads=[s_src], writes=[s_esb[pe]])
                    b.rw("scalar", lambda sc, j=j, pe=pe: sc.activation(
                        out=esb[:, pe, :], in_=esb[:, pe, :], func=AF.Exp),
                        reads=[s_esb[pe]], writes=[s_esb[pe]])
                    b.rw("vector", lambda v, j=j, pe=pe, src_t=src_t, dst_t=dst_t: v.scalar_tensor_tensor(
                        out=dst_t[:, j], in0=src_t[:, j], scalar=0.0, in1=esb[:, pe, :],
                        op0=ALU.max, op1=ALU.add),
                        reads=[s_src, s_esb[pe]], writes=[s_dst])
            for lt in range(4):
                for half in range(2):
                    pq = (2 * lt + half) % 2 + 2
                    for c in range(NC_DC):
                        b.rw("tensor", lambda e, lt=lt, half=half, c=c, pq=pq: e.matmul(
                            out=pa[pq][:], lhsT=xf[:, c, lt * 128:(lt + 1) * 128],
                            rhs=wv_sb[:, c, half * 512:(half + 1) * 512], start=(c == 0), stop=False),
                            reads=[s_xfo, s_wv], writes=[s_pa[pq]])
                    b.rw("tensor", lambda e, half=half, pq=pq: e.matmul(
                        out=pa[pq][:], lhsT=ones_col[:],
                        rhs=w_small["l0_bv"][0:1, half * 512:(half + 1) * 512], start=False, stop=True),
                        reads=[s_ones, s_small["l0_bv"]], writes=[s_pa[pq]])
                    b.rw("scalar", lambda sc, lt=lt, half=half, pq=pq: sc.copy(
                        out=vt[:, lt, half * 512:(half + 1) * 512], in_=pa[pq][:]),
                        reads=[s_pa[pq]], writes=[s_vt])
            for h in range(H):
                for ltpair in ((0, 1), (2, 3)):
                  for lt in ltpair:
                      w = 128 * (lt + 1)
                      pS = lt % 2
                      smv = sm[:, lt, :]
                      b.rw("tensor", lambda e, h=h, lt=lt, w=w, pS=pS: e.matmul(
                          out=pa[pS][:, 0:w], lhsT=qf[:, h, lt * 128:(lt + 1) * 128],
                          rhs=kf[:, h, 0:w], start=True, stop=True),
                          reads=[s_qf, s_kf], writes=[s_pa[pS]])
                      b.rw("vector", lambda v, lt=lt, w=w, pS=pS, smv=smv: v.reduce_max(
                          out=smv[:, 0:1], in_=pa[pS][:, 0:w], axis=AX.X),
                          reads=[s_pa[pS]], writes=[s_sm[lt]])
                      b.rw("vector", lambda v, smv=smv: v.tensor_scalar(
                          out=smv[:, 1:2], in0=smv[:, 0:1], scalar1=-INV, scalar2=0.0,
                          op0=ALU.mult, op1=ALU.add),
                          reads=[s_sm[lt]], writes=[s_sm[lt]])
                  for lt in ltpair:
                      w = 128 * (lt + 1)
                      pS = lt % 2
                      smv = sm[:, lt, :]
                      if lt > 0:
                          b.rw("scalar", lambda sc, lt=lt, pS=pS, smv=smv: sc.activation(
                              out=esb[:, lt, 0:128 * lt], in_=pa[pS][:, 0:128 * lt], func=AF.Exp,
                              bias=smv[:, 1:2], scale=INV, accum_out=smv[:, 2:3]),
                              reads=[s_pa[pS], s_sm[lt]], writes=[s_esb[lt], s_sm[lt]])
                      b.rw("scalar", lambda sc, lt=lt, w=w, pS=pS, smv=smv: sc.activation(
                          out=esb[:, lt, 128 * lt:w], in_=pa[pS][:, 128 * lt:w], func=AF.Exp,
                          bias=smv[:, 1:2], scale=INV),
                          reads=[s_pa[pS], s_sm[lt]], writes=[s_esb[lt]])
                for lt in range(4):
                    smv = sm[:, lt, :]
                    pQ = lt % 2
                    b.rw("tensor", lambda e, h=h, lt=lt, pQ=pQ: e.matmul(
                        out=pa[pQ][:, 0:129], lhsT=sqf[:, h, lt * 128:(lt + 1) * 128],
                        rhs=mbf[:, h], start=True, stop=True),
                        reads=[s_sqf, s_mbf], writes=[s_pa[pQ]])
                    b.rw("vector", lambda v, pQ=pQ, smv=smv: v.tensor_scalar(
                        out=smv[:, 6:7], in0=pa[pQ][:, 128:129], scalar1=EPS_ATT, scalar2=0.0,
                        op0=ALU.add, op1=ALU.add),
                        reads=[s_pa[pQ]], writes=[s_sm[lt]])
                    b.rw("vector", lambda v, smv=smv: v.reciprocal(out=smv[:, 7:8], in_=smv[:, 6:7]),
                        reads=[s_sm[lt]], writes=[s_sm[lt]])
                    b.rw("vector", lambda v, h=h, smv=smv: v.tensor_tensor(
                        out=smv[:, 8:9], in0=smv[:, 7:8], in1=g8[:, h:h + 1], op=ALU.mult),
                        reads=[s_sm[lt], s_g8], writes=[s_sm[lt]])
                    b.rw("scalar", lambda sc, lt=lt, pQ=pQ, smv=smv: sc.activation(
                        out=amg[:, lt, :], in_=pa[pQ][:, 0:128], func=AF.Identity, scale=smv[:, 8:9]),
                        reads=[s_pa[pQ], s_sm[lt]], writes=[s_amg[lt]])
                for lt in range(4):
                    smv = sm[:, lt, :]
                    pK = 2 if lt % 2 == 0 else 3
                    b.rw("tensor", lambda e, h=h, lt=lt, pK=pK: e.matmul(
                        out=pa[pK][:, 0:129], lhsT=skf[:, h, lt * 128:(lt + 1) * 128],
                        rhs=mbf[:, h], start=True, stop=True),
                        reads=[s_skf, s_mbf], writes=[s_pa[pK]])
                    b.rw("vector", lambda v, pK=pK, smv=smv: v.tensor_scalar(
                        out=smv[:, 9:10], in0=pa[pK][:, 128:129], scalar1=EPS_ATT, scalar2=0.0,
                        op0=ALU.add, op1=ALU.add),
                        reads=[s_pa[pK]], writes=[s_sm[lt]])
                    b.rw("vector", lambda v, smv=smv: v.reciprocal(out=smv[:, 10:11], in_=smv[:, 9:10]),
                        reads=[s_sm[lt]], writes=[s_sm[lt]])
                    b.rw("vector", lambda v, smv=smv: v.tensor_scalar(
                        out=smv[:, 11:12], in0=smv[:, 10:11], scalar1=-1.0, scalar2=0.0,
                        op0=ALU.mult, op1=ALU.add),
                        reads=[s_sm[lt]], writes=[s_sm[lt]])
                    b.rw("vector", lambda v, h=h, lt=lt, pK=pK, smv=smv: v.scalar_tensor_tensor(
                        out=vd[:, lt, 0:128], in0=pa[pK][:, 0:128], scalar=smv[:, 11:12],
                        in1=vt[:, lt, h * 128:(h + 1) * 128], op0=ALU.mult, op1=ALU.add),
                        reads=[s_pa[pK], s_sm[lt], s_vt], writes=[s_vd[lt]])
                    b.rw("tensor", lambda e, h=h, lt=lt: e.transpose(
                        out=pt[0][:], in_=skf[:, h, lt * 128:(lt + 1) * 128], identity=ident[:]),
                        reads=[s_skf, s_ident], writes=[s_pt[0]])
                    b.rw("scalar", lambda sc: sc.copy(out=skt[0][:], in_=pt[0][:]),
                        reads=[s_pt[0]], writes=[s_skt[0]])
                    b.rw("tensor", lambda e, h=h, lt=lt: e.matmul(
                        out=pmu[:, 0:129], lhsT=skt[0][:], rhs=vd[:, lt, :],
                        start=(lt == 0), stop=(lt == 3)),
                        reads=[s_skt[0], s_vd[lt], s_vd1], writes=[s_pmu])
                for lt in range(4):
                    w = 128 * (lt + 1)
                    smv = sm[:, lt, :]
                    b.rw("vector", lambda v, lt=lt, w=w, smv=smv: v.scalar_tensor_tensor(
                        out=esb[:, lt, 128 * lt:w], in0=esb[:, lt, 128 * lt:w], scalar=1.0,
                        in1=lmask[:], op0=ALU.mult, op1=ALU.mult, accum_out=smv[:, 3:4]),
                        reads=[s_esb[lt], s_lmask], writes=[s_esb[lt], s_sm[lt]])
                    if lt > 0:
                        b.rw("vector", lambda v, smv=smv: v.tensor_tensor(
                            out=smv[:, 3:4], in0=smv[:, 3:4], in1=smv[:, 2:3], op=ALU.add),
                            reads=[s_sm[lt]], writes=[s_sm[lt]])
                    b.rw("vector", lambda v, smv=smv: v.reciprocal(out=smv[:, 4:5], in_=smv[:, 3:4]),
                        reads=[s_sm[lt]], writes=[s_sm[lt]])
                    b.rw("vector", lambda v, h=h, smv=smv: v.tensor_tensor(
                        out=smv[:, 5:6], in0=smv[:, 4:5], in1=omg8[:, h:h + 1], op=ALU.mult),
                        reads=[s_sm[lt], s_omg8], writes=[s_sm[lt]])
                for lt in range(4):
                    smv = sm[:, lt, :]
                    pA = 2 + lt % 2
                    for mt in range(lt + 1):
                        pc = mt % 2
                        b.rw("tensor", lambda e, lt=lt, mt=mt, pc=pc: e.transpose(
                            out=pt[pc][:], in_=esb[:, lt, mt * 128:(mt + 1) * 128], identity=ident[:]),
                            reads=[s_esb[lt], s_ident], writes=[s_pt[pc]])
                        b.rw("scalar", lambda sc, mt=mt, pc=pc: sc.copy(
                            out=etb[0][:, mt], in_=pt[pc][:]),
                            reads=[s_pt[pc]], writes=[s_etb[0]])
                    for mt in range(lt + 1):
                        b.rw("tensor", lambda e, h=h, mt=mt, lt=lt, pA=pA: e.matmul(
                            out=pa[pA][:, 0:128], lhsT=etb[0][:, mt],
                            rhs=vt[:, mt, h * 128:(h + 1) * 128], start=(mt == 0), stop=(mt == lt)),
                            reads=[s_etb[0], s_vt], writes=[s_pa[pA]])
                    b.rw("vector", lambda v, h=h, lt=lt, pA=pA, smv=smv: v.scalar_tensor_tensor(
                        out=aout[:, lt, h * 128:(h + 1) * 128], in0=pa[pA][:, 0:128],
                        scalar=smv[:, 5:6], in1=amg[:, lt, :], op0=ALU.mult, op1=ALU.add),
                        reads=[s_pa[pA], s_sm[lt], s_amg[lt]], writes=[s_xfo])
                b.rw("vector", lambda v, h=h: v.tensor_tensor(
                    out=m32[:, h], in0=pmu[:, 0:129], in1=m32[:, h], op=ALU.add),
                    reads=[s_pmu, s_m32], writes=[s_m32])
                b.rw("vector", lambda v, h=h: v.tensor_copy(out=mbf[:, h], in_=m32[:, h]),
                    reads=[s_m32], writes=[s_mbf])
            for lt in range(4):
                t = 4 * s + lt
                b.rw("vector", lambda v, t=t, lt=lt: v.tensor_tensor(
                    out=stream[:, t], in0=aout[:, lt], in1=stream[:, t], op=ALU.add),
                    reads=[s_xfo, s_stream[t]], writes=[s_stream[t]])
        b.barrier()
        fs.close()

    GU_CH = [512, 512, 512, 512, 512, 256]
    GU_OFF = [0, 512, 1024, 1536, 2048, 2560]

    def emit_wquant():
        fs = ExitStack()
        fsb = lambda name, shape, dt=F32: fs.enter_context(nc.sbuf_tensor(name, shape, dt, side="right"))
        wst = [fsb(f"wq_st{i}", [128, I_FFN]) for i in range(2)]
        s_wst = [Buf(), Buf()]
        v32 = fsb("wq_v32", [128, I_FFN]); s_v32 = Buf()
        u32 = fsb("wq_u32", [128, I_FFN]); s_u32 = Buf()
        qb = fsb("wq_qb", [128, I_FFN], BF16); s_qb = Buf()
        qm = fsb("wq_qm", [128, I_FFN], BF16); s_qm = Buf()
        mk = fsb("wq_mk", [128, I_FFN], U8); s_mk = Buf()
        wqt = fsb("wq_wqt", [128, I_FFN], BF16); s_wqt = Buf()
        mx = fsb("wq_mx", [128, 176]); s_mx = Buf()
        scl = fsb("wq_scl", [128, 176]); s_scl = Buf()
        rscl = fsb("wq_rscl", [128, 176]); s_rscl = Buf()
        tst = [fsb(f"wq_tst{i}", [128, 128], BF16) for i in range(2)]
        s_tst = [Buf(), Buf()]

        def quant_tile(width, s_src, src_ap):
            nblk = width // 16
            b.rw("vector", lambda v: v.tensor_reduce(
                out=mx[:, 0:nblk], in_=src_ap.rearrange("p (n k) -> p n k", k=16),
                axis=AX.X, op=ALU.max, apply_absolute_value=True),
                reads=[s_src], writes=[s_mx])
            b.rw("vector", lambda v: v.tensor_scalar(
                out=scl[:, 0:nblk], in0=mx[:, 0:nblk], scalar1=1.0 / 6.0, scalar2=1e-12,
                op0=ALU.mult, op1=ALU.max), reads=[s_mx], writes=[s_scl])
            b.rw("vector", lambda v: v.reciprocal(out=rscl[:, 0:nblk], in_=scl[:, 0:nblk]),
                reads=[s_scl], writes=[s_rscl])
            rb = rscl[:, 0:nblk].rearrange("p (n o) -> p n o", o=1).broadcast_to([128, nblk, 16])
            b.rw("vector", lambda v, rb=rb: v.tensor_tensor(
                out=v32[:, 0:width].rearrange("p (n k) -> p n k", k=16), in0=rb,
                in1=src_ap.rearrange("p (n k) -> p n k", k=16), op=ALU.mult),
                reads=[s_rscl, s_src], writes=[s_v32])
            b.rw("scalar", lambda s: s.activation(out=u32[:, 0:width], in_=v32[:, 0:width], func=AF.Square),
                reads=[s_v32], writes=[s_u32])
            b.rw("vector", lambda v: v.tensor_scalar(
                out=qb[:, 0:width], in0=v32[:, 0:width], scalar1=M05, scalar2=M05,
                op0=ALU.add, op1=ALU.subtract), reads=[s_v32], writes=[s_qb])
            b.rw("vector", lambda v: v.tensor_scalar(
                out=qm[:, 0:width], in0=v32[:, 0:width], scalar1=M1, scalar2=M1,
                op0=ALU.add, op1=ALU.subtract), reads=[s_v32], writes=[s_qm])
            b.rw("vector", lambda v: v.tensor_single_scalar(
                out=mk[:, 0:width], in_=u32[:, 0:width], scalar=4.0, op=ALU.is_gt),
                reads=[s_u32], writes=[s_mk])
            b.rw("vector", lambda v: v.copy_predicated(
                out=qb[:, 0:width], mask=mk[:, 0:width], data=qm[:, 0:width]),
                reads=[s_mk, s_qm], writes=[s_qb])
            b.rw("vector", lambda v: v.tensor_scalar(
                out=qm[:, 0:width], in0=v32[:, 0:width], scalar1=M2, scalar2=M2,
                op0=ALU.add, op1=ALU.subtract), reads=[s_v32], writes=[s_qm])
            b.rw("vector", lambda v: v.tensor_single_scalar(
                out=mk[:, 0:width], in_=u32[:, 0:width], scalar=16.0, op=ALU.is_gt),
                reads=[s_u32], writes=[s_mk])
            b.rw("vector", lambda v: v.copy_predicated(
                out=qb[:, 0:width], mask=mk[:, 0:width], data=qm[:, 0:width]),
                reads=[s_mk, s_qm], writes=[s_qb])
            sbx = scl[:, 0:nblk].rearrange("p (n o) -> p n o", o=1).broadcast_to([128, nblk, 16])
            b.rw("vector", lambda v, sbx=sbx: v.tensor_tensor(
                out=wqt[:, 0:width].rearrange("p (n k) -> p n k", k=16), in0=sbx,
                in1=qb[:, 0:width].rearrange("p (n k) -> p n k", k=16), op=ALU.mult),
                reads=[s_scl, s_qb], writes=[s_wqt])

        for wname, ddst in (("f1_wg", d_wgq), ("f1_wu", d_wuq)):
            dflat = ddst.rearrange("(p c i) -> p c i", c=NC_DC, i=I_FFN)
            for it in range(22):
                pw = it % 2
                b.dma_rw("sync", wst[pw][:, 0:D], dw[wname][it * 128:(it + 1) * 128, :], writes=[s_wst[pw]])
                quant_tile(D, s_wst[pw], wst[pw][:, 0:D])
                for c in range(NC_DC):
                    pc = c % 2
                    b.rw("tensor", lambda e, c=c, pc=pc: e.transpose(
                        out=pt[pc][:], in_=wqt[:, c * 128:(c + 1) * 128], identity=ident[:]),
                        reads=[s_wqt, s_ident], writes=[s_pt[pc]])
                    b.rw("scalar", lambda s, c=c, pc=pc: s.copy(out=tst[pc][:], in_=pt[pc][:]),
                        reads=[s_pt[pc]], writes=[s_tst[pc]])
                    b.dma_rw("gpsimd", dflat[:, c, it * 128:(it + 1) * 128], tst[pc][:],
                             reads=[s_tst[pc]])
        for ot in range(NC_DC):
            pw = ot % 2
            b.dma_rw("sync", wst[pw][:], dw["f1_wd"][ot * 128:(ot + 1) * 128, :], writes=[s_wst[pw]])
            quant_tile(I_FFN, s_wst[pw], wst[pw][:])
            for cc in range(22):
                pc = cc % 2
                b.rw("tensor", lambda e, cc=cc, pc=pc: e.transpose(
                    out=pt[pc][:], in_=wqt[:, cc * 128:(cc + 1) * 128], identity=ident[:]),
                    reads=[s_wqt, s_ident], writes=[s_pt[pc]])
                b.rw("scalar", lambda s, cc=cc, pc=pc: s.copy(out=tst[pc][:], in_=pt[pc][:]),
                    reads=[s_pt[pc]], writes=[s_tst[pc]])
                b.dma_rw("gpsimd", d_wdq[cc, :, ot * 128:(ot + 1) * 128], tst[pc][:],
                         reads=[s_tst[pc]])
        b.barrier()
        fs.close()

    _f1_call = [0]

    def emit_ffn1():
        _f1_call[0] += 1
        sfx = f"_q{_f1_call[0]}"
        fs = ExitStack()
        fsb = lambda name, shape, dt=BF16: fs.enter_context(nc.sbuf_tensor(name + sfx, shape, dt, side="right"))
        wg_ch = [fsb(f"f1_wg{i}", [128, NC_DC, 512]) for i in range(2)]
        wu_ch = [fsb(f"f1_wu{i}", [128, NC_DC, 512]) for i in range(2)]
        hqT = fsb("f1_hqT", [128, 4, 22, 128]); s_hqT = Buf()
        hqs = fsb("f1_hqs", [128, 4, 512]); s_hqs = Buf()
        NWDR = 3
        wdr = fsb("f1_wdr", [128, NWDR, 512])
        s_wdr = [[Buf(), Buf()] for _ in range(NWDR)]
        s_wg = [Buf(), Buf()]; s_wu = [Buf(), Buf()]
        xq_fm = fsb("f1_xqfm", [128, NC_DC, 512]); s_xqfm = Buf()
        xh32 = fsb("f1_xh", [128, 2, D], F32); s_xh = Buf()
        xqs = [fsb(f"f1_xqs{i}", [128, 2, D]) for i in range(2)]
        s_xqs = [Buf(), Buf()]
        h32 = fsb("f1_h32", [128, 4, 512], F32); s_h32 = Buf()
        v16 = fsb("f1_v16", [128, 2048], FP16); s_v16 = Buf()
        u16 = fsb("f1_u16", [128, 2048], FP16); s_u16 = Buf()
        qb6 = fsb("f1_qb", [128, 2048]); s_qb6 = Buf()
        qm6 = fsb("f1_qm", [128, 2048]); s_qm6 = Buf()
        mk6 = fsb("f1_mk", [128, 2048], U8); s_mk6 = Buf()
        scls = fsb("f1_scls", [128, 128], F32); s_scls = Buf()
        rscls = fsb("f1_rscls", [128, 128], F32); s_rscls = Buf()

        def quant_batch(src4, s_src, dst4, s_dst, A, N):
            nblk = A * N
            W = nblk * 16
            b.rw("vector", lambda v: v.tensor_reduce(
                out=scls[:, 0:nblk].rearrange("p (a n) -> p a n", a=A), in_=src4,
                axis=AX.X, op=ALU.max,
                apply_absolute_value=True), reads=list(s_src), writes=[s_scls])
            b.rw("vector", lambda v: v.tensor_scalar(
                out=scls[:, 0:nblk], in0=scls[:, 0:nblk], scalar1=1.0 / 6.0, scalar2=1e-12,
                op0=ALU.mult, op1=ALU.max), reads=[s_scls], writes=[s_scls])
            b.rw("vector", lambda v: v.reciprocal(out=rscls[:, 0:nblk], in_=scls[:, 0:nblk]),
                reads=[s_scls], writes=[s_rscls])
            rb = rscls[:, 0:nblk].rearrange("p (a n o) -> p a n o", a=A, o=1).broadcast_to([128, A, N, 16])
            vv = v16[:, 0:W].rearrange("p (a n k) -> p a n k", a=A, k=16)
            b.rw("vector", lambda v, rb=rb, vv=vv: v.tensor_tensor(
                out=vv, in0=rb, in1=src4, op=ALU.mult),
                reads=[s_rscls] + list(s_src), writes=[s_v16])
            b.rw("vector", lambda v: v.tensor_tensor(
                out=u16[:, 0:W], in0=v16[:, 0:W], in1=v16[:, 0:W], op=ALU.mult),
                reads=[s_v16], writes=[s_u16])
            b.rw("vector", lambda v: v.tensor_scalar(
                out=qb6[:, 0:W], in0=v16[:, 0:W], scalar1=M05, scalar2=M05,
                op0=ALU.add, op1=ALU.subtract), reads=[s_v16], writes=[s_qb6])
            b.rw("vector", lambda v: v.tensor_scalar(
                out=qm6[:, 0:W], in0=v16[:, 0:W], scalar1=M1, scalar2=M1,
                op0=ALU.add, op1=ALU.subtract), reads=[s_v16], writes=[s_qm6])
            b.rw("vector", lambda v: v.tensor_single_scalar(
                out=mk6[:, 0:W], in_=u16[:, 0:W], scalar=4.0, op=ALU.is_gt),
                reads=[s_u16], writes=[s_mk6])
            b.rw("vector", lambda v: v.copy_predicated(
                out=qb6[:, 0:W], mask=mk6[:, 0:W], data=qm6[:, 0:W]),
                reads=[s_mk6, s_qm6], writes=[s_qb6])
            b.rw("vector", lambda v: v.tensor_scalar(
                out=qm6[:, 0:W], in0=v16[:, 0:W], scalar1=M2, scalar2=M2,
                op0=ALU.add, op1=ALU.subtract), reads=[s_v16], writes=[s_qm6])
            b.rw("vector", lambda v: v.tensor_single_scalar(
                out=mk6[:, 0:W], in_=u16[:, 0:W], scalar=16.0, op=ALU.is_gt),
                reads=[s_u16], writes=[s_mk6])
            b.rw("vector", lambda v: v.copy_predicated(
                out=qb6[:, 0:W], mask=mk6[:, 0:W], data=qm6[:, 0:W]),
                reads=[s_mk6, s_qm6], writes=[s_qb6])
            sbx = scls[:, 0:nblk].rearrange("p (a n o) -> p a n o", a=A, o=1).broadcast_to([128, A, N, 16])
            qv = qb6[:, 0:W].rearrange("p (a n k) -> p a n k", a=A, k=16)
            b.rw("vector", lambda v, sbx=sbx, qv=qv: v.tensor_tensor(
                out=dst4, in0=sbx, in1=qv, op=ALU.mult),
                reads=[s_scls, s_qb6], writes=list(s_dst))

        dgq = d_wgq.rearrange("(p c i) -> p c i", c=NC_DC, i=I_FFN)
        duq = d_wuq.rearrange("(p c i) -> p c i", c=NC_DC, i=I_FFN)
        emit_ln_stats([qb6[:, 0:D], qb6[:, D:2 * D]], [s_qb6, s_qb6],
                      [qm6[:, 0:D], qm6[:, D:2 * D]], [s_qm6, s_qm6])
        xq_st = [None]

        def emit_xquant(st):
            for half in range(2):
                for i2 in range(2):
                    t = 4 * st + 2 * half + i2
                    emit_xhat(t, xh32[:, i2, :], s_xh)
                quant_batch(xh32[:].rearrange("p a (n k) -> p a n k", k=16), [s_xh],
                            xqs[half][:].rearrange("p a (n k) -> p a n k", k=16), [s_xqs[half]], 2, 64)

        def emit_xtrans(st):
            for half in range(2):
                for i2 in range(2):
                    i = 2 * half + i2
                    for c in range(NC_DC):
                        pc = c % 2
                        b.rw("tensor", lambda e, c=c, i2=i2, half=half, pc=pc: e.transpose(
                            out=pt[pc][:], in_=xqs[half][:, i2, c * 128:(c + 1) * 128], identity=ident[:]),
                            reads=[s_xqs[half], s_ident], writes=[s_pt[pc]])
                        b.rw("scalar", lambda s, c=c, i=i, pc=pc: s.copy(
                            out=xq_fm[:, c, i * 128:(i + 1) * 128], in_=pt[pc][:]),
                            reads=[s_pt[pc]], writes=[s_xqfm])

        emit_xquant(0)
        for st in range(8):
            emit_xtrans(st)
            pend_T = []
            for ic in range(6):
                w = GU_CH[ic]; i0 = GU_OFF[ic]
                pw = ic % 2
                b.dma_rw("sync", wg_ch[pw][:, :, 0:w], dgq[:, :, i0:i0 + w], writes=[s_wg[pw]])
                b.dma_rw("sync", wu_ch[pw][:, :, 0:w], duq[:, :, i0:i0 + w], writes=[s_wu[pw]])
                for tt in range(4):
                    ph = 0
                    pg, pu = 2 * (tt % 2), 2 * (tt % 2) + 1
                    for c in range(NC_DC):
                        b.rw("tensor", lambda e, c=c, tt=tt, w=w, pw=pw, pg=pg: e.matmul(
                            out=pa[pg][:, 0:w], lhsT=xq_fm[:, c, tt * 128:(tt + 1) * 128],
                            rhs=wg_ch[pw][:, c, 0:w], start=(c == 0), stop=(c == NC_DC - 1)),
                            reads=[s_xqfm, s_wg[pw]], writes=[s_pa[pg]])
                    for c in range(NC_DC):
                        b.rw("tensor", lambda e, c=c, tt=tt, w=w, pw=pw, pu=pu: e.matmul(
                            out=pa[pu][:, 0:w], lhsT=xq_fm[:, c, tt * 128:(tt + 1) * 128],
                            rhs=wu_ch[pw][:, c, 0:w], start=(c == 0), stop=(c == NC_DC - 1)),
                            reads=[s_xqfm, s_wu[pw]], writes=[s_pa[pu]])
                    b.rw("scalar", lambda s, w=w, tt=tt, pg=pg: s.activation(
                        out=h32[:, tt, 0:w], in_=pa[pg][:, 0:w], func=AF.Silu),
                        reads=[s_pa[pg]], writes=[s_h32])
                    b.rw("vector", lambda v, w=w, tt=tt, pu=pu: v.tensor_tensor(
                        out=h32[:, tt, 0:w],
                        in0=h32[:, tt, 0:w], in1=pa[pu][:, 0:w], op=ALU.mult),
                        reads=[s_h32, s_pa[pu]], writes=[s_h32])
                for f in pend_T:
                    f()
                pend_T = []
                quant_batch(h32[:, :, 0:w].rearrange("p a (n k) -> p a n k", k=16), [s_h32],
                            hqs[:, :, 0:w].rearrange("p a (n k) -> p a n k", k=16),
                            [s_hqs], 4, w // 16)
                cc0 = i0 // 128
                for tt in range(4):
                    for j in range(w // 128):
                        pc = (tt * (w // 128) + j) % 2
                        def emit_T(tt=tt, j=j, pc=pc, cc=cc0 + j):
                            b.rw("tensor", lambda e, tt=tt, j=j, pc=pc: e.transpose(
                                out=pt[pc][:], in_=hqs[:, tt, j * 128:(j + 1) * 128], identity=ident[:]),
                                reads=[s_hqs, s_ident], writes=[s_pt[pc]])
                            b.rw("scalar", lambda s, tt=tt, cc=cc, pc=pc: s.copy(
                                out=hqT[:, tt, cc], in_=pt[pc][:]),
                                reads=[s_pt[pc]], writes=[s_hqT])
                        pend_T.append(emit_T)
            for f in pend_T:
                f()
            if st + 1 < 8:
                emit_xquant(st + 1)
            for o in range(2):
                def wd_dma(cc, o=o):
                    b.dma_rw("gpsimd", wdr[:, cc % NWDR, 0:256],
                             d_wdq[cc, :, o * 512:o * 512 + 256],
                             writes=[s_wdr[cc % NWDR][0]])
                    b.dma_rw("scalar", wdr[:, cc % NWDR, 256:512],
                             d_wdq[cc, :, o * 512 + 256:o * 512 + 512],
                             writes=[s_wdr[cc % NWDR][1]])
                for cc in range(NWDR):
                    wd_dma(cc)
                for cc in range(22):
                    sl = cc % NWDR
                    for tt in range(4):
                        b.rw("tensor", lambda e, cc=cc, tt=tt, sl=sl: e.matmul(
                            out=pa[tt][:], lhsT=hqT[:, tt, cc], rhs=wdr[:, sl],
                            start=(cc == 0), stop=(cc == 21)),
                            reads=[s_hqT, s_wdr[sl][0], s_wdr[sl][1]], writes=[s_pa[tt]])
                    if cc + NWDR < 22:
                        wd_dma(cc + NWDR)
                for tt in range(4):
                    t = 4 * st + tt
                    b.rw("vector", lambda v, t=t, tt=tt, o=o: v.scalar_tensor_tensor(
                        out=stream[:, t, o * 512:(o + 1) * 512], in0=pa[tt][:],
                        scalar=0.5, in1=stream[:, t, o * 512:(o + 1) * 512],
                        op0=ALU.mult, op1=ALU.add),
                        reads=[s_pa[tt], s_stream[t]], writes=[s_stream[t]])
        b.barrier()
        fs.close()

    def emit_l1attn():
        H1, HD = 16, 64
        fs = ExitStack()
        fsb = lambda name, shape, dt=BF16: fs.enter_context(nc.sbuf_tensor(name, shape, dt, side="right"))
        w2 = fsb("l1_w2", [128, NC_DC, 2048]); s_w2 = Buf()
        kvE = fsb("l1_kvE", [128, 8, 65], F32); s_kvE = Buf()
        kvO = fsb("l1_kvO", [128, 8, 65], F32); s_kvO = Buf()
        kv2b = fsb("l1_kv2b", [128, 8, 130]); s_kv2b = Buf()
        axh = [fsb(f"l1_xh{i}", [128, D]) for i in range(2)]
        s_axh = [Buf(), Buf()]
        xf_t = [fsb(f"l1_xf{i}", [128, NC_DC, 128]) for i in range(2)]
        s_xft = [Buf(), Buf()]
        fk_tm = [fsb(f"l1_fk{i}", [128, D]) for i in range(2)]
        s_fk = [Buf(), Buf()]
        v_tm = [fsb(f"l1_v{i}", [128, 8, 130]) for i in range(2)]
        s_v = [Buf(), Buf()]
        s_vones = [Buf(), Buf()]
        scr32 = [fsb(f"l1_scr{i}", [128, 512], F32) for i in range(2)]
        s_scr = [Buf(), Buf()]
        xf_g = fsb("l1_xfg", [128, NC_DC, 512]); s_xfg = Buf()
        fq_g = fsb("l1_fqg", [128, NC_DC, 512]); s_fqg = Buf()
        out_tm = fsb("l1_out", [128, 4, D]); s_out = Buf()
        ofm = [fsb(f"l1_ofm{i}", [128, NC_DC, 128]) for i in range(2)]
        s_ofm = [Buf(), Buf()]
        sm1 = fsb("l1_sm", [128, 8], F32); s_sm1 = Buf()

        def load_w2(names):
            for wi, wname in enumerate(names):
                for c in range(NC_DC):
                    b.dma_rw("sync", stage32[:], dw[wname][c * 128:(c + 1) * 128, :], writes=[s_stg[0]])
                    b.rw("vector", lambda v, c=c, wi=wi: v.tensor_copy(
                        out=w2[:, c, wi * 1024:(wi + 1) * 1024], in_=stage32[:]),
                        reads=[s_stg[0]], writes=[s_w2])

        emit_ln_stats([fk_tm[0][:], fk_tm[1][:]], s_fk, [axh[0][:], axh[1][:]], s_axh)
        load_w2(["l1_wkt", "l1_wvt"])
        b.rw("gpsimd", lambda g: g.memset(kvE[:], 0.0), writes=[s_kvE])
        b.rw("gpsimd", lambda g: g.memset(kvO[:], 0.0), writes=[s_kvO])
        for p in range(2):
            b.rw("gpsimd", lambda g, p=p: g.memset(v_tm[p][:, :, 64:65], 1.0), writes=[s_vones[p]])
            b.rw("gpsimd", lambda g, p=p: g.memset(v_tm[p][:, :, 129:130], 1.0), writes=[s_vones[p]])

        for t in range(NT):
            p = t % 2
            emit_xhat(t, axh[p][:], s_axh[p])
            for c in range(NC_DC):
                pc = c % 2
                b.rw("tensor", lambda e, c=c, p=p, pc=pc: e.transpose(
                    out=pt[pc][:], in_=axh[p][:, c * 128:(c + 1) * 128], identity=ident[:]),
                    reads=[s_axh[p], s_ident], writes=[s_pt[pc]])
                b.rw("scalar", lambda s, c=c, p=p, pc=pc: s.copy(
                    out=xf_t[p][:, c], in_=pt[pc][:]),
                    reads=[s_pt[pc]], writes=[s_xft[p]])
            for half in range(2):
                for c in range(NC_DC):
                    b.rw("tensor", lambda e, c=c, p=p, half=half: e.matmul(
                        out=pa[half][:], lhsT=xf_t[p][:, c],
                        rhs=w2[:, c, half * 512:(half + 1) * 512], start=(c == 0), stop=False),
                        reads=[s_xft[p], s_w2], writes=[s_pa[half]])
                b.rw("tensor", lambda e, half=half: e.matmul(
                    out=pa[half][:], lhsT=ones_col[:],
                    rhs=w_small["l1_bk"][0:1, half * 512:(half + 1) * 512], start=False, stop=True),
                    reads=[s_ones, s_small["l1_bk"]], writes=[s_pa[half]])
                b.rw("vector", lambda v, half=half, p=p: v.tensor_single_scalar(
                    out=scr32[half][:], in_=pa[half][:], scalar=0.0, op=ALU.min),
                    reads=[s_pa[half]], writes=[s_scr[half]])
                b.rw("scalar", lambda s, half=half: s.activation(
                    out=scr32[half][:], in_=scr32[half][:], func=AF.Exp),
                    reads=[s_scr[half]], writes=[s_scr[half]])
                b.rw("vector", lambda v, half=half, p=p: v.scalar_tensor_tensor(
                    out=fk_tm[p][:, half * 512:(half + 1) * 512], in0=pa[half][:], scalar=0.0,
                    in1=scr32[half][:], op0=ALU.max, op1=ALU.add),
                    reads=[s_pa[half], s_scr[half]], writes=[s_fk[p]])
                for c in range(NC_DC):
                    b.rw("tensor", lambda e, c=c, p=p, half=half: e.matmul(
                        out=pa[2 + half][:], lhsT=xf_t[p][:, c],
                        rhs=w2[:, c, 1024 + half * 512:1024 + (half + 1) * 512], start=(c == 0), stop=False),
                        reads=[s_xft[p], s_w2], writes=[s_pa[2 + half]])
                b.rw("tensor", lambda e, half=half: e.matmul(
                    out=pa[2 + half][:], lhsT=ones_col[:],
                    rhs=w_small["l1_bv"][0:1, half * 512:(half + 1) * 512], start=False, stop=True),
                    reads=[s_ones, s_small["l1_bv"]], writes=[s_pa[2 + half]])
                vsrc = pa[2 + half][:].rearrange("q (a two m) -> q a two m", two=2, m=64)
                b.rw("scalar", lambda s, half=half, p=p, vsrc=vsrc: s.copy(
                    out=v_tm[p][:, 4 * half:4 * half + 4, 0:64], in_=vsrc[:, :, 0, :]),
                    reads=[s_pa[2 + half]], writes=[s_v[p]])
                b.rw("scalar", lambda s, half=half, p=p, vsrc=vsrc: s.copy(
                    out=v_tm[p][:, 4 * half:4 * half + 4, 65:129], in_=vsrc[:, :, 1, :]),
                    reads=[s_pa[2 + half]], writes=[s_v[p]])
            for j in range(8):
                pj = j % 2
                dstp = pnk if pj == 0 else pmu
                s_dstp = s_pnk if pj == 0 else s_pmu
                b.rw("tensor", lambda e, j=j, p=p, dstp=dstp: e.matmul(
                    out=dstp[:], lhsT=fk_tm[p][:, j * 128:(j + 1) * 128],
                    rhs=v_tm[p][:, j, :], start=True, stop=True),
                    reads=[s_fk[p], s_v[p], s_vones[p]], writes=[s_dstp])
                b.rw("vector", lambda v, j=j, dstp=dstp: v.tensor_tensor(
                    out=kvE[0:64, j, :], in0=dstp[0:64, 0:65], in1=kvE[0:64, j, :], op=ALU.add),
                    reads=[s_dstp, s_kvE], writes=[s_kvE])
                b.rw("vector", lambda v, j=j, dstp=dstp: v.tensor_tensor(
                    out=kvO[64:128, j, :], in0=dstp[64:128, 65:130], in1=kvO[64:128, j, :], op=ALU.add),
                    reads=[s_dstp, s_kvO], writes=[s_kvO])

        load_w2(["l1_wqt", "l1_wot"])
        b.rw("gpsimd", lambda g: g.memset(kv2b[:], 0.0), writes=[s_kv2b])
        b.rw("vector", lambda v: v.tensor_copy(out=kv2b[0:64, :, 0:65], in_=kvE[0:64, :, :]),
             reads=[s_kvE], writes=[s_kv2b])
        b.rw("vector", lambda v: v.tensor_copy(out=kv2b[64:128, :, 65:130], in_=kvO[64:128, :, :]),
             reads=[s_kvO], writes=[s_kv2b])
        for g in range(8):
            for i in range(4):
                t = 4 * g + i
                p = i % 2
                emit_xhat(t, axh[p][:], s_axh[p])
                for c in range(NC_DC):
                    pc = c % 2
                    b.rw("tensor", lambda e, c=c, p=p, pc=pc: e.transpose(
                        out=pt[pc][:], in_=axh[p][:, c * 128:(c + 1) * 128], identity=ident[:]),
                        reads=[s_axh[p], s_ident], writes=[s_pt[pc]])
                    b.rw("scalar", lambda s, c=c, i=i, pc=pc: s.copy(
                        out=xf_g[:, c, i * 128:(i + 1) * 128], in_=pt[pc][:]),
                        reads=[s_pt[pc]], writes=[s_xfg])
            for j in range(NC_DC):
                pj = j % 2
                for c in range(NC_DC):
                    b.rw("tensor", lambda e, j=j, c=c, pj=pj: e.matmul(
                        out=pa[pj][:], lhsT=w2[:, c, j * 128:(j + 1) * 128],
                        rhs=xf_g[:, c, :], start=(c == 0), stop=False),
                        reads=[s_w2, s_xfg], writes=[s_pa[pj]])
                b.rw("tensor", lambda e, j=j, pj=pj: e.matmul(
                    out=pa[pj][:], lhsT=w_small["l1_bq"][0:1, j * 128:(j + 1) * 128],
                    rhs=ones_row[:], start=False, stop=True),
                    reads=[s_small["l1_bq"], s_onesr], writes=[s_pa[pj]])
                b.rw("vector", lambda v, pj=pj: v.tensor_single_scalar(
                    out=scr32[pj][:], in_=pa[pj][:], scalar=0.0, op=ALU.min),
                    reads=[s_pa[pj]], writes=[s_scr[pj]])
                b.rw("scalar", lambda s, pj=pj: s.activation(
                    out=scr32[pj][:], in_=scr32[pj][:], func=AF.Exp),
                    reads=[s_scr[pj]], writes=[s_scr[pj]])
                b.rw("vector", lambda v, j=j, pj=pj: v.scalar_tensor_tensor(
                    out=fq_g[:, j, :], in0=pa[pj][:], scalar=0.0,
                    in1=scr32[pj][:], op0=ALU.max, op1=ALU.add),
                    reads=[s_pa[pj], s_scr[pj]], writes=[s_fqg])
            for tt in range(4):
                for j in range(NC_DC):
                    pj = (4 * tt + j) % 2 + 2
                    b.rw("tensor", lambda e, j=j, tt=tt, pj=pj: e.matmul(
                        out=pa[pj][:, 0:130], lhsT=fq_g[:, j, tt * 128:(tt + 1) * 128],
                        rhs=kv2b[:, j, :], start=True, stop=True),
                        reads=[s_fqg, s_kv2b], writes=[s_pa[pj]])
                    for parity in range(2):
                        h = 2 * j + parity
                        dcol = 64 if parity == 0 else 129
                        vcol = 0 if parity == 0 else 65
                        b.rw("vector", lambda v, pj=pj, dcol=dcol, parity=parity: v.tensor_scalar(
                            out=sm1[:, parity:parity + 1], in0=pa[pj][:, dcol:dcol + 1],
                            scalar1=EPS_ATT, scalar2=0.0, op0=ALU.max, op1=ALU.add),
                            reads=[s_pa[pj]], writes=[s_sm1])
                        b.rw("vector", lambda v, parity=parity: v.reciprocal(
                            out=sm1[:, 2 + parity:3 + parity], in_=sm1[:, parity:parity + 1]),
                            reads=[s_sm1], writes=[s_sm1])
                        b.rw("scalar", lambda s, tt=tt, h=h, pj=pj, vcol=vcol, parity=parity: s.activation(
                            out=out_tm[:, tt, h * 64:(h + 1) * 64], in_=pa[pj][:, vcol:vcol + 64],
                            func=AF.Identity, scale=sm1[:, 2 + parity:3 + parity]),
                            reads=[s_pa[pj], s_sm1], writes=[s_out])
            for tt in range(4):
                t = 4 * g + tt
                po = tt % 2
                for c in range(NC_DC):
                    pc = c % 2
                    b.rw("tensor", lambda e, c=c, tt=tt, pc=pc: e.transpose(
                        out=pt[pc][:], in_=out_tm[:, tt, c * 128:(c + 1) * 128], identity=ident[:]),
                        reads=[s_out, s_ident], writes=[s_pt[pc]])
                    b.rw("scalar", lambda s, c=c, po=po, pc=pc: s.copy(
                        out=ofm[po][:, c], in_=pt[pc][:]),
                        reads=[s_pt[pc]], writes=[s_ofm[po]])
                for o in range(2):
                    b.rw("tensor", lambda e, o=o, po=po, c=0: e.matmul(
                        out=pa[o][:], lhsT=ofm[po][:, 0],
                        rhs=w2[:, 0, 1024 + o * 512:1024 + (o + 1) * 512], start=True, stop=False),
                        reads=[s_ofm[po], s_w2], writes=[s_pa[o]])
                    for c in range(1, NC_DC):
                        b.rw("tensor", lambda e, c=c, o=o, po=po: e.matmul(
                            out=pa[o][:], lhsT=ofm[po][:, c],
                            rhs=w2[:, c, 1024 + o * 512:1024 + (o + 1) * 512], start=False, stop=False),
                            reads=[s_ofm[po], s_w2], writes=[s_pa[o]])
                    b.rw("tensor", lambda e, o=o: e.matmul(
                        out=pa[o][:], lhsT=ones_col[:],
                        rhs=w_small["l1_bo"][0:1, o * 512:(o + 1) * 512], start=False, stop=True),
                        reads=[s_ones, s_small["l1_bo"]], writes=[s_pa[o]])
                    b.rw("vector", lambda v, t=t, o=o: v.tensor_tensor(
                        out=stream[:, t, o * 512:(o + 1) * 512], in0=pa[o][:],
                        in1=stream[:, t, o * 512:(o + 1) * 512], op=ALU.add),
                        reads=[s_pa[o], s_stream[t]], writes=[s_stream[t]])
        b.barrier()
        fs.close()

    def emit_hamprobe():
        fs = ExitStack()
        hw8 = fs.enter_context(nc.sbuf_tensor("hp_w8", [128, 8, 128], BF16, side="right"))
        ha = fs.enter_context(nc.sbuf_tensor("hp_a", [128, 512], BF16, side="right"))
        s_hw = Buf(); s_ha = Buf()
        b.rw("gpsimd", lambda g: g.memset(hw8[:], 0.5), writes=[s_hw])
        b.rw("gpsimd", lambda g: g.memset(ha[:], 0.5), writes=[s_ha])
        for g in range(125):
            pp = g % 2
            for c in range(NC_DC):
                b.rw("tensor", lambda e, c=c, pp=pp: e.matmul(
                    out=pa[pp][:], lhsT=hw8[:, c], rhs=ha[:], start=(c == 0), stop=(c == 7)),
                    reads=[s_hw, s_ha], writes=[s_pa[pp]])
        b.barrier()
        fs.close()

    if "hamprobe" in phases:
        emit_hamprobe()
    if "ffn0a" in phases:
        emit_ffn0()
    if "l0attn" in phases:
        emit_l0attn()
    if "ffn0b" in phases:
        emit_ffn0()
    if "wq" in phases:
        emit_wquant()
    if "ffn1a" in phases:
        emit_ffn1()
    if "l1attn" in phases:
        emit_l1attn()
    if "ffn1b" in phases:
        emit_ffn1()

    emit_output()
    return b


def make_shared(inputs):
    f32c = lambda a: np.ascontiguousarray(np.asarray(a, np.float32))
    return {
        "l0_wq": f32c(inputs["L0_attn_Wq"]), "l0_wk": f32c(inputs["L0_attn_Wk"]),
        "l0_wv": f32c(inputs["L0_attn_Wv"]),
        "l0_bq": f32c(inputs["L0_attn_bq"]).reshape(1, -1),
        "l0_bk": f32c(inputs["L0_attn_bk"]).reshape(1, -1),
        "l0_bv": f32c(inputs["L0_attn_bv"]).reshape(1, -1),
        "l0_beta": f32c(inputs["L0_attn_beta"]).reshape(1, -1),
        "f0_win": f32c(inputs["L0_ffn_Win"]),
        "f0_bin": f32c(inputs["L0_ffn_bin_"]).reshape(1, -1),
        "f0_w1": f32c(inputs["L0_ffn_W1"]).reshape(1, -1),
        "f0_b1": f32c(inputs["L0_ffn_b1"]).reshape(1, -1),
        "f0_w2": f32c(inputs["L0_ffn_W2"]).reshape(1, -1),
        "f0_b2": f32c(inputs["L0_ffn_b2"]).reshape(1, -1),
        "f0_wout": f32c(inputs["L0_ffn_Wout"]),
        "f0_bout": f32c(inputs["L0_ffn_bout"]).reshape(1, -1),
        "l1_wqt": f32c(inputs["L1_attn_Wq"].T), "l1_wkt": f32c(inputs["L1_attn_Wk"].T),
        "l1_wvt": f32c(inputs["L1_attn_Wv"].T), "l1_wot": f32c(inputs["L1_attn_Wo"].T),
        "l1_bq": f32c(inputs["L1_attn_bq"]).reshape(1, -1),
        "l1_bk": f32c(inputs["L1_attn_bk"]).reshape(1, -1),
        "l1_bv": f32c(inputs["L1_attn_bv"]).reshape(1, -1),
        "l1_bo": f32c(inputs["L1_attn_bo"]).reshape(1, -1),
        "f1_wg": f32c(inputs["L1_ffn_w_gate"]), "f1_wu": f32c(inputs["L1_ffn_w_up"]),
        "f1_wd": f32c(inputs["L1_ffn_w_down"]),
    }


def kernel(**inputs):
    xin = np.ascontiguousarray(inputs["L0_attn_x"].astype(np.float32))
    shared = make_shared(inputs)
    import os
    phases = tuple(os.environ.get("KPHASES", "ffn0a,l0attn,ffn0b,wq,ffn1a,l1attn,ffn1b").split(","))
    nc = bass.Bass("TRN2", target_bir_lowering=False)
    b = build(nc, phases)
    with ExitStack() as st:
        block = st.enter_context(nc.Block())
        b.finalize(block)
    b.close()
    in_maps = [dict(shared, x=np.ascontiguousarray(xin[i])) for i in range(8)]
    res = run_bass_kernel_spmd(nc, in_maps, core_ids=list(range(8)))
    out = np.stack([np.asarray(r["out"], np.float32) for r in res.results], axis=0)
    return out
